# version 09
# baseline (speedup 1.0000x reference)
import os
import numpy as np
import concourse.bass as bass
import concourse.mybir as mybir
from concourse.bass_utils import run_bass_kernel_spmd
from contextlib import ExitStack

F32 = mybir.dt.float32
BF16 = mybir.dt.bfloat16
AF = mybir.ActivationFunctionType
ALU = mybir.AluOpType
AX = mybir.AxisListType

DEBUG = False
PHASES = 3

S, D, H, NE = 8192, 1024, 2816, 8
NHC = H // 128
NT = S // 128
NCH = S // 512
NDC = D // 128
EPS = 1e-5
NEG = -1e30
CSH = -45.0


class Sem:
    __slots__ = ("h", "n")

    def __init__(self, st, nc, name):
        self.h = st.enter_context(nc.semaphore(name))
        self.n = 0


class Guard:
    __slots__ = ("w",)

    def __init__(self):
        self.w = ()

    def need(self):
        return self.w

    def set(self, *pairs):
        self.w = tuple(pairs)


class B:
    def __init__(self):
        self.streams = {k: [] for k in ("sync", "gpsimd", "tensor", "vector", "scalar")}

    def op(self, eng, fn, waits=(), incs=()):
        for s, v in waits:
            assert v <= s.n, f"[{eng}] wait {v} > emitted {s.n}"
        caps_w = tuple((s.h, int(v)) for s, v in waits if v > 0)
        caps_i = []
        for s, a in incs:
            s.n += a
            caps_i.append((s.h, int(a)))
        caps_i = tuple(caps_i)

        def run(E, _w=caps_w, _f=fn, _i=caps_i):
            for h, v in _w:
                E.wait_ge(h, v)
            ins = _f(E)
            for h, a in _i:
                ins.then_inc(h, a)

        self.streams[eng].append(run)

    def wait(self, eng, *pairs):
        for s, v in pairs:
            assert v <= s.n, f"[{eng}] wait {v} > emitted {s.n}"
        caps = tuple((s.h, int(v)) for s, v in pairs if v > 0)

        def run(E, _w=caps):
            for h, v in _w:
                E.wait_ge(h, v)

        self.streams[eng].append(run)


def build_nc():
    nc = bass.Bass("TRN2", target_bir_lowering=False, debug=False,
                   detect_race_conditions=False)
    io_kind = "ExternalOutput" if DEBUG else "Internal"

    x_d = nc.dram_tensor("x", [S, D], F32, kind="ExternalInput").ap()
    wq_d = nc.dram_tensor("wq", [D, D], F32, kind="ExternalInput").ap()
    wk_d = nc.dram_tensor("wk", [D, D], F32, kind="ExternalInput").ap()
    wv_d = nc.dram_tensor("wv", [D, D], F32, kind="ExternalInput").ap()
    wg_d = nc.dram_tensor("wg", [NE, D], F32, kind="ExternalInput").ap()
    w1_d = nc.dram_tensor("w1", [NE, H, D], F32, kind="ExternalInput").ap()
    w2_d = nc.dram_tensor("w2", [NE, D, H], F32, kind="ExternalInput").ap()
    w3_d = nc.dram_tensor("w3", [NE, H, D], F32, kind="ExternalInput").ap()
    out_d = nc.dram_tensor("out", [S, D], F32, kind="ExternalOutput").ap()

    xnT_d = nc.dram_tensor("xnT_s", [NDC, 128, S], BF16, kind=io_kind).ap()
    xnTlo_d = nc.dram_tensor("xnTlo_s", [NDC, 128, S], BF16, kind=io_kind).ap()
    qT_d = nc.dram_tensor("qT_s", [NDC, 128, S], BF16, kind=io_kind).ap()
    kT_d = nc.dram_tensor("kT_s", [NDC, 128, S], BF16, kind=io_kind).ap()
    v_d = nc.dram_tensor("v_s", [NT, 128, D], BF16, kind=io_kind).ap()
    attn_d = nc.dram_tensor("attn_s", [NT, 128, D], BF16, kind=io_kind).ap()
    cw_d = nc.dram_tensor("cw_s", [128, NT * 8], F32, kind=io_kind).ap()
    sdump_d = nc.dram_tensor("sdump", [NT, 128, 16], F32, kind=io_kind).ap()
    s2dump_d = nc.dram_tensor("s2dump", [NT, 128, 32], F32, kind=io_kind).ap()
    t0dump_d = nc.dram_tensor("t0dump", [128, 1024], F32, kind=io_kind).ap()
    t0pdump_d = nc.dram_tensor("t0pdump", [128, 512], BF16, kind=io_kind).ap()
    t0o_d = nc.dram_tensor("t0o", [128, 1024], F32, kind=io_kind).ap()
    t0pt_d = nc.dram_tensor("t0pt", [128, 256], BF16, kind=io_kind).ap()
    t0v2_d = nc.dram_tensor("t0v2", [128, 4096], BF16, kind=io_kind).ap()
    w1cm_d = nc.dram_tensor("w1cm", [NE, NDC, H, 128], BF16, kind="Internal").ap()
    w3cm_d = nc.dram_tensor("w3cm", [NE, NDC, H, 128], BF16, kind="Internal").ap()
    w2cm_d = nc.dram_tensor("w2cm", [NE, NHC, D, 128], BF16, kind="Internal").ap()

    st = ExitStack()
    E_ = st.enter_context

    id32 = E_(nc.sbuf_tensor("id32", [128, 128], F32)).ap()
    idbf = E_(nc.sbuf_tensor("idbf", [128, 128], BF16)).ap()
    cw_sb = E_(nc.sbuf_tensor("cw_sb", [128, NT * 8], F32)).ap()
    wgT_hi = E_(nc.sbuf_tensor("wgT_hi", [128, NDC * 8], BF16)).ap()
    wgT_lo = E_(nc.sbuf_tensor("wgT_lo", [128, NDC * 8], BF16)).ap()
    stats = E_(nc.sbuf_tensor("stats", [128, 64], F32)).ap()
    tp2 = E_(nc.sbuf_tensor("tp2", [128, 512], F32)).ap()
    wg_sb = E_(nc.sbuf_tensor("wg_sb", [8, D], F32)).ap()

    ARENA = 188928
    arena_h = E_(nc.sbuf_tensor("arena", [128, ARENA], mybir.dt.uint8))
    arena_addr = nc.lookup_mloc(arena_h).addr

    def place(name, shape, dtype, off):
        h = nc.virtual_sbuf_tensor(name, shape, dtype)
        nc._tensor(h.name, list(h.shape), h.dtype, type="SB")
        mloc = nc.lookup_mloc(h)
        mloc.allocated = True
        mloc.addr = arena_addr + off
        return h.ap()

    masks = place("masks", [128, 4 * 512], F32, 180736)
    wq_sb = place("wq_sb", [128, NDC * D], BF16, 0)
    wk_sb = place("wk_sb", [128, NDC * D], BF16, 16384)
    wv_sb = place("wv_sb", [128, NDC * D], BF16, 32768)
    x_sb = place("x_sb", [128, 4 * D], F32, 49152)
    xnf_sb = place("xnf_sb", [128, 8 * D], F32, 65536)
    xnT_sb = place("xnT_sb", [128, 2 * NDC * 512], BF16, 98304)
    xnTlo_sb = place("xnTlo_sb", [128, 2 * NDC * 512], BF16, 114688)
    xnTv = place("xnTv", [128, 2, NDC, 4, 128], BF16, 98304)
    xnTlov = place("xnTlov", [128, 2, NDC, 4, 128], BF16, 114688)
    qT_sb = place("qT_sb", [128, 2 * NDC * 512], BF16, 131072)
    kT_sb = place("kT_sb", [128, 2 * NDC * 512], BF16, 147456)
    v_sb = place("v_sb", [128, 4 * D], BF16, 163840)
    wg_sb2 = None
    lgT_sb = place("lgT_sb", [8, 512], F32, 172032)
    kt_res = place("kt_res", [128, NDC * S], BF16, 0)
    sg_sb = place("sg_sb", [128, 512], F32, 131072)
    v2_sb = place("v2_sb", [128, 3 * 4096], BF16, 133120)
    qt_sb = place("qt_sb", [128, 2 * 1024], BF16, 157696)
    p_sb = place("p_sb", [128, 2 * 512], BF16, 161792)
    pt_sb = place("pt_sb", [128, 3 * 512], BF16, 163840)
    att_sb = place("att_sb", [128, 2 * 1024], BF16, 166912)
    w1t_sb = place("w1t_sb", [128, NDC * H], BF16, 0)
    w3t_sb = place("w3t_sb", [128, NDC * H], BF16, 45056)
    w2t_sb = place("w2t_sb", [128, NHC * D], BF16, 90112)
    ht_sb = place("ht_sb", [128, NHC * 512], BF16, 135168)
    xn3_sb = place("xn3_sb", [128, 2 * NDC * 512], BF16, 157696)
    y_sb = place("y_sb", [128, 2 * D], F32, 174080)
    att3_sb = place("att3_sb", [128, 2 * D], BF16, 182272)
    sil_sb = place("sil_sb", [128, 512], F32, 186368)

    pA = E_(nc.psum_tensor("pA", [128, 512], F32)).ap()
    pB = E_(nc.psum_tensor("pB", [128, 512], F32)).ap()
    pC = E_(nc.psum_tensor("pC", [128, 512], F32)).ap()
    pD = E_(nc.psum_tensor("pD", [128, 512], F32)).ap()
    pE = E_(nc.psum_tensor("pE", [128, 1024], F32)).ap()
    pG = E_(nc.psum_tensor("pG", [128, 512], BF16)).ap()
    pH = E_(nc.psum_tensor("pH", [128, 512], BF16)).ap()

    sems = {}
    for name in ("sSetup", "sWld", "sWgld", "sWgT", "sWgHi", "sWgLo",
                 "sXin", "sVAR", "sRSTD", "sNMR", "sXNF", "sSQ",
                 "sTR", "sHI", "sLO", "sLGC", "sLG", "sLGE", "sT2V", "sT2S", "sCW",
                 "sQT", "sQTE", "sQTD", "sKT", "sKTE", "sKTD", "sV", "sVE", "sVD",
                 "sXNTD", "sXNTLOD", "sWcast", "sDBG", "sPASS",
                 "sKTR", "sVL", "sQTL", "sS", "sSC", "sMN", "sAL", "sORS", "sAVB",
                 "sWT", "sXNL", "sYL", "sH1", "sH3", "sSIL", "sMULT", "sYM", "sYE", "sYD",
                 "sT2A", "sT2B", "sRECX", "sREC2",
                 "sP", "sPT", "sPTE", "sAV", "sREC", "sATT", "sATTD"):
        sems[name] = Sem(st, nc, name)
    g = type("G", (), sems)

    b = B()

    gpA, gpB, gpC, gpD = Guard(), Guard(), Guard(), Guard()
    gpE5, gpE6 = Guard(), Guard()
    gpG, gpH = Guard(), Guard()
    gx = [Guard() for _ in range(4)]
    gxnf = [Guard() for _ in range(8)]
    gxnT = [Guard(), Guard()]
    gxnTlo = [Guard(), Guard()]
    gqT = [Guard(), Guard()]
    gkT = [Guard(), Guard()]
    gv = Guard()
    glgT = Guard()

    b.op("gpsimd", lambda E: E.memset(cw_sb[:, :], 0.0))
    b.op("gpsimd", lambda E: E.memset(stats[:, :], 0.0))
    b.op("gpsimd", lambda E: E.memset(stats[:, 9:10], CSH), incs=[(g.sSetup, 1)])
    b.op("gpsimd", lambda E: E.memset(tp2[:, :], 0.0))
    b.op("gpsimd", lambda E: E.memset(id32, 0.0))
    b.op("gpsimd", lambda E: E.affine_select(out=id32, in_=id32, compare_op=ALU.not_equal,
                                             fill=1.0, base=0, pattern=[[-1, 128]],
                                             channel_multiplier=1), incs=[(g.sSetup, 1)])
    b.op("gpsimd", lambda E: E.memset(idbf, 0.0))
    b.op("gpsimd", lambda E: E.affine_select(out=idbf, in_=idbf, compare_op=ALU.not_equal,
                                             fill=1.0, base=0, pattern=[[-1, 128]],
                                             channel_multiplier=1), incs=[(g.sSetup, 1)])
    for p in range(4):
        m = masks[:, p * 512:(p + 1) * 512]
        b.op("gpsimd", lambda E, a=m: E.memset(a, 0.0))
        b.op("gpsimd", lambda E, a=m, pp=p: E.affine_select(
            out=a, in_=a, compare_op=ALU.is_ge, fill=NEG, base=128 * pp,
            pattern=[[-1, 512]], channel_multiplier=1), incs=[(g.sSetup, 1)])

    for w_dram, w_sb in ((wq_d, wq_sb), (wk_d, wk_sb), (wv_d, wv_sb)):
        for dc in range(NDC):
            b.op("gpsimd", lambda E, src=w_dram[dc * 128:(dc + 1) * 128, :],
                 dst=w_sb[:, dc * D:(dc + 1) * D]: E.dma_start(out=dst, in_=src),
                 incs=[(g.sWld, 16)])
    b.op("sync", lambda E: E.dma_start(out=wg_sb, in_=wg_d), incs=[(g.sWgld, 16)])
    WLD_FULL = g.sWld.n

    b.wait("tensor", (g.sWld, WLD_FULL), (g.sWgld, 16), (g.sSetup, 7))
    for dc in range(NDC):
        gd = gpA if dc % 2 == 0 else gpB
        ps = (pA if dc % 2 == 0 else pB)[:, 0:8]
        b.op("tensor", lambda E, p_=ps, src=wg_sb[0:8, dc * 128:(dc + 1) * 128]:
             E.transpose(p_, src, id32[0:8, 0:8]), waits=gd.need(), incs=[(g.sWgT, 1)])
        b.op("scalar", lambda E, p_=ps, d_=wgT_hi[:, dc * 8:(dc + 1) * 8]:
             E.copy(d_, p_), waits=[(g.sWgT, dc + 1)], incs=[(g.sWgHi, 1)])
        b.op("vector", lambda E, p_=ps, h_=wgT_hi[:, dc * 8:(dc + 1) * 8],
             d_=wgT_lo[:, dc * 8:(dc + 1) * 8]:
             E.tensor_tensor(d_, p_, h_, ALU.subtract),
             waits=[(g.sWgHi, dc + 1)], incs=[(g.sWgLo, 1)])
        gd.set((g.sWgLo, dc + 1))

    for e in range(NE):
        for dc in range(NDC):
            b.op("gpsimd", lambda E, d_=w1cm_d[e, dc], s_=w1_d[e, :, dc * 128:(dc + 1) * 128]:
                 E.dma_start(out=d_, in_=s_), incs=[(g.sWcast, 16)])
            b.op("gpsimd", lambda E, d_=w3cm_d[e, dc], s_=w3_d[e, :, dc * 128:(dc + 1) * 128]:
                 E.dma_start(out=d_, in_=s_), incs=[(g.sWcast, 16)])
        for hc in range(NHC):
            b.op("gpsimd", lambda E, d_=w2cm_d[e, hc], s_=w2_d[e, :, hc * 128:(hc + 1) * 128]:
                 E.dma_start(out=d_, in_=s_), incs=[(g.sWcast, 16)])
    WCAST_FULL = g.sWcast.n

    ntr = 0
    tile_tr_end = {}

    def tr_slot(k):
        return (gpA, pA) if k % 2 == 0 else (gpB, pB)


    def p1_ln(c):
        T0 = 4 * c
        xss, xns = [], []
        for tl in range(4):
            T = T0 + tl
            xs = x_sb[:, (T % 4) * D:(T % 4 + 1) * D]
            xn = xnf_sb[:, (T % 8) * D:(T % 8 + 1) * D]
            xss.append(xs)
            xns.append(xn)
            b.op("sync", lambda E, d_=xs, s_=x_d[T * 128:(T + 1) * 128, :]:
                 E.dma_start(out=d_, in_=s_), waits=gx[T % 4].need(),
                 incs=[(g.sXin, 16)])
        for tl in range(4):
            b.op("vector", lambda E, s_=xss[tl], i=tl: E.reduce_sum(
                stats[:, i:i + 1], s_, axis=AX.X),
                waits=[(g.sXin, (T0 + tl + 1) * 16)])
        for tl in range(4):
            T = T0 + tl
            b.op("scalar", lambda E, s_=xss[tl], d_=xns[tl], i=tl: E.activation(
                d_, s_, AF.Square, accum_out=stats[:, 4 + i:5 + i]),
                waits=[(g.sXin, (T + 1) * 16)] + list(gxnf[T % 8].need()),
                incs=[(g.sSQ, 1)])
        SQ_C = g.sSQ.n
        for tl in range(4):
            b.op("vector", lambda E, i=tl: E.tensor_scalar_mul(
                stats[:, 10 + i:11 + i], stats[:, i:i + 1], 1.0 / D))
        for tl in range(4):
            b.op("vector", lambda E, i=tl: E.tensor_scalar_mul(
                stats[:, 14 + i:15 + i], stats[:, 4 + i:5 + i], 1.0 / D),
                waits=([(g.sSQ, SQ_C)] if tl == 0 else []))
        for tl in range(4):
            b.op("vector", lambda E, i=tl: E.tensor_mul(
                stats[:, 18 + i:19 + i], stats[:, 10 + i:11 + i],
                stats[:, 10 + i:11 + i]))
        for tl in range(4):
            b.op("vector", lambda E, i=tl: E.tensor_scalar(
                stats[:, 22 + i:23 + i], stats[:, 14 + i:15 + i],
                stats[:, 18 + i:19 + i], EPS, ALU.subtract, ALU.add),
                incs=([(g.sVAR, 1)] if tl == 3 else []))
        for tl in range(4):
            b.op("scalar", lambda E, i=tl: E.activation(
                stats[:, 26 + i:27 + i], stats[:, 22 + i:23 + i], AF.Ln),
                waits=([(g.sVAR, c + 1)] if tl == 0 else []))
        for tl in range(4):
            b.op("scalar", lambda E, i=tl: E.activation(
                stats[:, 30 + i:31 + i], stats[:, 26 + i:27 + i], AF.Exp, scale=-0.5),
                incs=([(g.sRSTD, 1)] if tl == 3 else []))
        for tl in range(4):
            b.op("vector", lambda E, i=tl: E.scalar_tensor_tensor(
                stats[:, 34 + i:35 + i], stats[:, 10 + i:11 + i], -1.0,
                stats[:, 30 + i:31 + i], ALU.mult, ALU.mult),
                waits=([(g.sRSTD, c + 1)] if tl == 0 else []),
                incs=([(g.sNMR, 1)] if tl == 3 else []))
        for tl in range(4):
            b.op("scalar", lambda E, s_=xss[tl], d_=xns[tl], i=tl: E.activation(
                d_, s_, AF.Identity, scale=stats[:, 30 + i:31 + i],
                bias=stats[:, 34 + i:35 + i]),
                waits=([(g.sNMR, c + 1)] if tl == 0 else []), incs=[(g.sXNF, 1)])
            gx[(T0 + tl) % 4].set((g.sXNF, g.sXNF.n))

    def p1_tr(c):
        nonlocal ntr
        for tl in range(4):
            T = 4 * c + tl
            xn = xnf_sb[:, (T % 8) * D:(T % 8 + 1) * D]
            b.wait("tensor", (g.sXNF, T + 1))
            for half in range(2):
                gd, pp = tr_slot(ntr)
                wt = list(gd.need())
                if tl == 0 and half == 0:
                    wt += list(gxnT[c % 2].need())
                    wt += list(gxnTlo[c % 2].need())
                for q in range(4):
                    dc = half * 4 + q
                    b.op("tensor", lambda E, p_=pp[:, q * 128:(q + 1) * 128],
                         s_=xn[:, dc * 128:(dc + 1) * 128]:
                         E.transpose(p_, s_, id32), waits=(wt if q == 0 else ()),
                         incs=([(g.sTR, 1)] if q == 3 else ()))
                hi = xnTv[:, c % 2, half * 4:half * 4 + 4, tl, :]
                lo = xnTlov[:, c % 2, half * 4:half * 4 + 4, tl, :]
                b.op("scalar", lambda E, p_=pp[:, 0:512], d_=hi: E.copy(d_, p_),
                     waits=[(g.sTR, g.sTR.n)], incs=[(g.sHI, 1)])
                b.op("vector", lambda E, p_=pp[:, 0:512], h_=hi, d_=lo:
                     E.tensor_tensor(d_, p_, h_, ALU.subtract),
                     waits=[(g.sHI, g.sHI.n)], incs=[(g.sLO, 1)])
                gd.set((g.sHI, g.sHI.n), (g.sLO, g.sLO.n))
                ntr += 1
            gxnf[T % 8].set((g.sTR, g.sTR.n))

    def p1_chunk(c):
        nonlocal ntr
        hi0 = (c % 2) * NDC * 512

        def xh(dc):
            return xnT_sb[:, hi0 + dc * 512: hi0 + (dc + 1) * 512]

        def xl(dc):
            return xnTlo_sb[:, hi0 + dc * 512:hi0 + (dc + 1) * 512]

        terms = []
        for dc in range(NDC):
            terms.append((wgT_hi[:, dc * 8:(dc + 1) * 8], xh(dc)))
            terms.append((wgT_hi[:, dc * 8:(dc + 1) * 8], xl(dc)))
            terms.append((wgT_lo[:, dc * 8:(dc + 1) * 8], xh(dc)))
        wt0 = [(g.sHI, g.sHI.n), (g.sLO, g.sLO.n)] + list(gpC.need())
        for i, (lh, rh) in enumerate(terms):
            b.op("tensor", lambda E, l_=lh, r_=rh, s0=(i == 0), s1=(i == len(terms) - 1):
                 E.matmul(pC[0:8, :], l_, r_, start=s0, stop=s1),
                 waits=(wt0 if i == 0 else ()),
                 incs=([(g.sLG, 1)] if i == len(terms) - 1 else ()))
        b.op("scalar", lambda E: E.copy(lgT_sb[:, :], pC[0:8, :]),
             waits=[(g.sLG, c + 1)] + list(glgT.need()), incs=[(g.sLGE, 1)])
        gpC.set((g.sLGE, g.sLGE.n))

        LG, P8, MK1, MSD, MK2 = 0, 32, 64, 96, 128
        NM1, M2C, ZC, DEN, REC, REC2 = 160, 164, 168, 172, 176, 180
        for tl in range(4):
            gd, pp = tr_slot(ntr)
            ps = pp[:, 0:8]
            b.op("tensor", lambda E, p_=ps, s_=lgT_sb[0:8, tl * 128:(tl + 1) * 128]:
                 E.transpose(p_, s_, id32[0:8, 0:8]),
                 waits=[(g.sLGE, c + 1)] + list(gd.need()), incs=[(g.sTR, 1)])
            b.op("vector", lambda E, p_=ps, d_=tp2[:, LG + tl * 8:LG + (tl + 1) * 8]:
                 E.tensor_copy(d_, p_),
                 waits=[(g.sTR, ntr + 1)], incs=[(g.sLGC, 1)])
            gd.set((g.sLGC, g.sLGC.n))
            ntr += 1
        for tl in range(4):
            b.op("vector", lambda E, i=tl: E.reduce_max(
                tp2[:, NM1 + i:NM1 + i + 1], tp2[:, LG + i * 8:LG + (i + 1) * 8],
                axis=AX.X, negate=True), incs=([(g.sT2A, 1)] if tl == 3 else ()))
        b.wait("scalar", (g.sT2A, g.sT2A.n))
        for tl in range(4):
            b.op("scalar", lambda E, i=tl: E.activation(
                tp2[:, P8 + i * 8:P8 + (i + 1) * 8], tp2[:, LG + i * 8:LG + (i + 1) * 8],
                AF.Exp, bias=tp2[:, NM1 + i:NM1 + i + 1]),
                incs=([(g.sT2S, 1)] if tl == 3 else ()))
        b.wait("vector", (g.sT2S, g.sT2S.n))
        for tl in range(4):
            b.op("vector", lambda E, i=tl: E.tensor_scalar(
                tp2[:, MK1 + i * 8:MK1 + (i + 1) * 8], tp2[:, P8 + i * 8:P8 + (i + 1) * 8],
                1.0, 0.0, ALU.subtract, ALU.is_ge))
        b.op("vector", lambda E: E.tensor_copy(tp2[:, 256:512], masks[:, 0:256]))
        for tl in range(4):
            b.op("vector", lambda E, i=tl: E.scalar_tensor_tensor(
                tp2[:, MSD + i * 8:MSD + (i + 1) * 8], tp2[:, MK1 + i * 8:MK1 + (i + 1) * 8],
                NEG, tp2[:, LG + i * 8:LG + (i + 1) * 8], ALU.mult, ALU.add))
        b.op("vector", lambda E: E.tensor_copy(tp2[:, 256:512], masks[:, 0:256]))
        for tl in range(4):
            b.op("vector", lambda E, i=tl: E.reduce_max(
                tp2[:, M2C + i:M2C + i + 1], tp2[:, MSD + i * 8:MSD + (i + 1) * 8],
                axis=AX.X), incs=([(g.sT2B, 1)] if tl == 3 else ()))
        b.wait("scalar", (g.sT2B, g.sT2B.n))
        for tl in range(4):
            b.op("scalar", lambda E, i=tl: E.activation(
                tp2[:, ZC + i:ZC + i + 1], tp2[:, M2C + i:M2C + i + 1], AF.Exp,
                bias=tp2[:, NM1 + i:NM1 + i + 1]),
                incs=([(g.sT2S, 1)] if tl == 3 else ()))
        b.wait("vector", (g.sT2S, g.sT2S.n))
        for tl in range(4):
            b.op("vector", lambda E, i=tl: E.tensor_scalar_add(
                tp2[:, DEN + i:DEN + i + 1], tp2[:, ZC + i:ZC + i + 1], 1.0))
        for tl in range(4):
            b.op("vector", lambda E, i=tl: E.tensor_scalar(
                tp2[:, MK2 + i * 8:MK2 + (i + 1) * 8], tp2[:, P8 + i * 8:P8 + (i + 1) * 8],
                tp2[:, ZC + i:ZC + i + 1], 0.0, ALU.subtract, ALU.is_ge))
        for tl in range(4):
            b.op("vector", lambda E, i=tl: E.reciprocal(
                tp2[:, REC + i:REC + i + 1], tp2[:, DEN + i:DEN + i + 1]),
                incs=([(g.sRECX, 1)] if tl == 3 else ()))
        b.wait("scalar", (g.sRECX, g.sRECX.n))
        for tl in range(4):
            b.op("scalar", lambda E, i=tl: E.copy(
                tp2[:, REC2 + i:REC2 + i + 1], tp2[:, REC + i:REC + i + 1]),
                incs=([(g.sREC2, 1)] if tl == 3 else ()))
        b.wait("vector", (g.sREC2, g.sREC2.n))
        for tl in range(4):
            T = 4 * c + tl
            b.op("vector", lambda E, i=tl, d_=cw_sb[:, T * 8:(T + 1) * 8]:
                 E.scalar_tensor_tensor(d_, tp2[:, P8 + i * 8:P8 + (i + 1) * 8],
                                        tp2[:, REC2 + i:REC2 + i + 1],
                                        tp2[:, MK2 + i * 8:MK2 + (i + 1) * 8],
                                        ALU.mult, ALU.mult), incs=[(g.sCW, 1)])
        glgT.set((g.sLGC, g.sLGC.n))

        for (w_sb, sMM, sME, sMD, dram, dst_sb, gslot, scale) in (
                (wq_sb, g.sQT, g.sQTE, g.sQTD, qT_d, qT_sb, gqT, 1.0 / 32.0),
                (wk_sb, g.sKT, g.sKTE, g.sKTD, kT_d, kT_sb, gkT, None)):
            for do in range(NDC):
                gd = gpC if do % 2 == 0 else gpD
                ps = pC if do % 2 == 0 else pD
                for di in range(NDC):
                    lh = w_sb[:, di * D + do * 128: di * D + (do + 1) * 128]
                    b.op("tensor", lambda E, p_=ps, l_=lh, r_=xh(di),
                         s0=(di == 0), s1=(di == NDC - 1):
                         E.matmul(p_, l_, r_, start=s0, stop=s1),
                         waits=(gd.need() if di == 0 else ()),
                         incs=([(sMM, 1)] if di == NDC - 1 else ()))
                dsoff = (c % 2) * NDC * 512 + do * 512
                dst = dst_sb[:, dsoff:dsoff + 512]
                wsl = list(gslot[c % 2].need()) if do == 0 else []
                if scale is None:
                    b.op("scalar", lambda E, p_=ps, d_=dst: E.copy(d_, p_),
                         waits=[(sMM, sMM.n)] + wsl, incs=[(sME, 1)])
                else:
                    b.op("scalar", lambda E, p_=ps, d_=dst, sc_=scale: E.activation(
                        d_, p_, AF.Copy, scale=sc_), waits=[(sMM, sMM.n)] + wsl,
                        incs=[(sME, 1)])
                gd.set((sME, sME.n))
                b.op("sync", lambda E, s_=dst, d_=dram[do, :, c * 512:(c + 1) * 512]:
                     E.dma_start(out=d_, in_=s_), waits=[(sME, sME.n)], incs=[(sMD, 16)])
            gslot[c % 2].set((sMD, sMD.n))

        for ts in range(4):
            T = 4 * c + ts
            pairs = (((gpE5, pE[:, 0:512]), (gpE6, pE[:, 512:1024])) if ts % 2 == 0
                     else ((gpC, pC), (gpD, pD)))
            for dh in range(2):
                gd, ps = pairs[dh]
                for di in range(NDC):
                    lh = xnT_sb[:, hi0 + di * 512 + ts * 128: hi0 + di * 512 + ts * 128 + 128]
                    rh = wv_sb[:, di * D + dh * 512: di * D + (dh + 1) * 512]
                    b.op("tensor", lambda E, p_=ps, l_=lh, r_=rh,
                         s0=(di == 0), s1=(di == NDC - 1):
                         E.matmul(p_, l_, r_, start=s0, stop=s1),
                         waits=(gd.need() if di == 0 else ()),
                         incs=([(g.sV, 1)] if di == NDC - 1 else ()))
                voff = ts * D + dh * 512
                dst = v_sb[:, voff:voff + 512]
                wsl = list(gv.need()) if (ts == 0 and dh == 0) else []
                b.op("scalar", lambda E, p_=ps, d_=dst: E.copy(d_, p_),
                     waits=[(g.sV, g.sV.n)] + wsl, incs=[(g.sVE, 1)])
                gd.set((g.sVE, g.sVE.n))
            src = v_sb[:, ts * D:(ts + 1) * D]
            b.op("sync", lambda E, s_=src, d_=v_d[T]: E.dma_start(out=d_, in_=s_),
                 waits=[(g.sVE, g.sVE.n)], incs=[(g.sVD, 16)])
        gv.set((g.sVD, g.sVD.n))

        for dc in range(NDC):
            b.op("sync", lambda E, s_=xh(dc), d_=xnT_d[dc, :, c * 512:(c + 1) * 512]:
                 E.dma_start(out=d_, in_=s_), waits=[(g.sHI, g.sHI.n)],
                 incs=[(g.sXNTD, 16)])
        gxnT[c % 2].set((g.sV, g.sV.n), (g.sXNTD, g.sXNTD.n), (g.sLG, g.sLG.n))
        gxnTlo[c % 2].set((g.sLG, g.sLG.n))

    b.wait("tensor", (g.sWgLo, 8), (g.sSetup, 7))
    for c in range(NCH):
        p1_ln(c)
        if c >= 1:
            p1_tr(c - 1)
        if c >= 2:
            p1_chunk(c - 2)
    p1_tr(NCH - 1)
    p1_chunk(NCH - 2)
    p1_chunk(NCH - 1)

    if PHASES >= 2:
        P1_DONE = [(g.sV, g.sV.n), (g.sXNF, g.sXNF.n), (g.sLG, g.sLG.n),
                   (g.sQT, g.sQT.n), (g.sKT, g.sKT.n), (g.sQTE, g.sQTE.n),
                   (g.sKTE, g.sKTE.n), (g.sVE, g.sVE.n), (g.sHI, g.sHI.n),
                   (g.sLO, g.sLO.n), (g.sLGC, g.sLGC.n), (g.sCW, g.sCW.n),
                   (g.sQTD, g.sQTD.n), (g.sKTD, g.sKTD.n), (g.sVD, g.sVD.n),
                   (g.sXNTD, g.sXNTD.n), (g.sXNTLOD, g.sXNTLOD.n)]
        for dc in range(NDC):
            b.op("sync", lambda E, d_=kt_res[:, dc * S:(dc + 1) * S], s_=kT_d[dc]:
                 E.dma_start(out=d_, in_=s_),
                 waits=(P1_DONE if dc == 0 else ()), incs=[(g.sKTR, 16)])
        b.wait("gpsimd", *P1_DONE)

        gqt = [Guard(), Guard()]
        gv2 = [Guard(), Guard(), Guard()]
        gps2 = [Guard(), Guard()]
        gpt = [Guard(), Guard(), Guard()]
        gatt = [Guard(), Guard()]
        gsg = Guard()
        gpEav = Guard()
        pte_mark = {}
        vl_mark = {}
        rec_mark = {}
        att_mark = {}

        def emit_s(i, ck, Ci, QTL_I, lbase):
            gd, pp = (gpA, pA) if ck % 2 == 0 else (gpB, pB)
            wt = list(gd.need())
            if ck == 0:
                wt += [(g.sQTL, QTL_I), (g.sKTR, 8 * 16)]
            for dc in range(NDC):
                lh = qt_sb[:, (i % 2) * 1024 + dc * 128:(i % 2) * 1024 + (dc + 1) * 128]
                rh = kt_res[:, dc * S + ck * 512: dc * S + (ck + 1) * 512]
                b.op("tensor", lambda E, p_=pp, l_=lh, r_=rh,
                     s0=(dc == 0), s1=(dc == NDC - 1):
                     E.matmul(p_, l_, r_, start=s0, stop=s1),
                     waits=(wt if dc == 0 else ()),
                     incs=([(g.sS, 1)] if dc == NDC - 1 else ()))
            b.op("sync", lambda E, d_=v2_sb[:, (ck % 3) * 4096:(ck % 3 + 1) * 4096],
                 s_=v_d[4 * ck:4 * ck + 4].rearrange("j p d -> p j d"):
                 E.dma_start(out=d_, in_=s_), waits=list(gv2[ck % 3].need()),
                 incs=[(g.sVL, 16)])
            vl_mark[ck] = g.sVL.n
            lcol = stats[:, lbase + ck:lbase + ck + 1]
            wtE = list(gps2[ck % 2].need())
            if i >= 2:
                wtE += [(g.sREC, rec_mark[i - 2])]
            if ck == Ci - 1:
                mp = masks[:, (i % 4) * 512:(i % 4 + 1) * 512]
                b.op("vector", lambda E, p_=pp, d_=sg_sb[:, 0:512], m_=mp:
                     E.tensor_tensor(d_, p_, m_, ALU.add),
                     waits=[(g.sS, g.sS.n)] + list(gsg.need()), incs=[(g.sSC, 1)])
                b.op("scalar", lambda E, d_=p_sb[:, (ck % 2) * 512:(ck % 2 + 1) * 512],
                     a_=lcol: E.activation(d_, sg_sb[:, 0:512], AF.Exp,
                                           bias=stats[:, 9:10], accum_out=a_),
                     waits=[(g.sSC, g.sSC.n)] + wtE, incs=[(g.sP, 1)])
                gd.set((g.sSC, g.sSC.n))
                gsg.set((g.sP, g.sP.n))
            else:
                b.op("scalar", lambda E, p_=pp,
                     d_=p_sb[:, (ck % 2) * 512:(ck % 2 + 1) * 512],
                     a_=lcol: E.activation(d_, p_, AF.Exp, bias=stats[:, 9:10],
                                           accum_out=a_),
                     waits=[(g.sS, g.sS.n)] + wtE, incs=[(g.sP, 1)])
                gd.set((g.sP, g.sP.n))

        def emit_t(i, ck):
            gdq, pq = (gpG, pG) if ck % 2 == 0 else (gpH, pH)
            for blk in range(4):
                wtT = [(g.sP, g.sP.n)] + (list(gdq.need()) if blk == 0 else [])
                b.op("tensor", lambda E, p_=pq[:, blk * 128:(blk + 1) * 128],
                     s_=p_sb[:, (ck % 2) * 512 + blk * 128:(ck % 2) * 512 + (blk + 1) * 128]:
                     E.transpose(p_, s_, idbf), waits=wtT,
                     incs=([(g.sPT, 1)] if blk == 3 else ()))
            b.op("vector", lambda E, p_=pq[:, 0:512],
                 d_=pt_sb[:, (ck % 3) * 512:(ck % 3 + 1) * 512]:
                 E.tensor_copy(d_, p_),
                 waits=[(g.sPT, g.sPT.n)] + list(gpt[ck % 3].need()),
                 incs=[(g.sPTE, 1)])
            pte_mark[ck] = g.sPTE.n
            gdq.set((g.sPTE, g.sPTE.n))
            gps2[ck % 2].set((g.sPT, g.sPT.n))

        def emit_av(i, ck, Ci):
            slot = ck % 3
            first = (ck == 0)
            last = (ck == Ci - 1)
            pm, vm = pte_mark[ck], vl_mark[ck]
            for blk in range(4):
                for dh in range(2):
                    wt2 = ([(g.sPTE, pm), (g.sVL, vm)]
                           if (blk == 0 and dh == 0) else [])
                    if first and blk == 0:
                        wt2 = wt2 + list(gpEav.need())
                    b.op("tensor", lambda E,
                         p_=(pE[:, 0:512] if dh == 0 else pE[:, 512:1024]),
                         l_=pt_sb[:, slot * 512 + blk * 128:
                                  slot * 512 + (blk + 1) * 128],
                         r_=v2_sb[:, slot * 4096 + blk * 1024 + dh * 512:
                                  slot * 4096 + blk * 1024 + (dh + 1) * 512],
                         s0=(first and blk == 0), s1=(last and blk == 3):
                         E.matmul(p_, l_, r_, start=s0, stop=s1),
                         waits=wt2,
                         incs=([(g.sAV, 1)] if dh == 1 else []))
            gpt[slot].set((g.sAV, g.sAV.n))
            gv2[slot].set((g.sAV, g.sAV.n))

        for i in range(min(NT, NCH * 4)):
            Ci = i // 4 + 1
            lbase = 32 + (i % 2) * 16
            b.op("sync", lambda E, d_=qt_sb[:, (i % 2) * 1024:(i % 2 + 1) * 1024],
                 s_=qT_d[:, :, i * 128:(i + 1) * 128].rearrange("c p t -> p c t"):
                 E.dma_start(out=d_, in_=s_), waits=list(gqt[i % 2].need()),
                 incs=[(g.sQTL, 16)])
            QTL_I = g.sQTL.n
            for ck in range(Ci):
                emit_s(i, ck, Ci, QTL_I, lbase)
                if ck >= 1:
                    emit_t(i, ck - 1)
                if ck >= 2:
                    emit_av(i, ck - 2, Ci)
            emit_t(i, Ci - 1)
            if Ci >= 2:
                emit_av(i, Ci - 2, Ci)
            emit_av(i, Ci - 1, Ci)
            b.op("vector", lambda E, n_=Ci, o_=lbase: E.reduce_sum(
                stats[:, 27:28], stats[:, o_:o_ + n_], axis=AX.X),
                waits=[(g.sP, g.sP.n)])
            b.op("vector", lambda E: E.tensor_copy(tp2[:, 256:512], masks[:, 0:256]))
            wtR = [(g.sATT, att_mark[i - 2])] if i >= 2 else []
            b.op("vector", lambda E, c_=30 + (i % 2): E.reciprocal(
                stats[:, c_:c_ + 1], stats[:, 27:28]), waits=wtR)
            b.op("vector", lambda E: E.tensor_copy(tp2[:, 256:512], masks[:, 0:256]),
                 incs=[(g.sREC, 1)])
            rec_mark[i] = g.sREC.n
            wt = [(g.sREC, g.sREC.n), (g.sAV, g.sAV.n)] + list(gatt[i % 2].need())
            b.op("scalar", lambda E, d_=att_sb[:, (i % 2) * 1024:(i % 2 + 1) * 1024],
                 c_=30 + (i % 2):
                 E.activation(d_, pE[:, :], AF.Copy, scale=stats[:, c_:c_ + 1]),
                 waits=wt, incs=[(g.sATT, 1)])
            att_mark[i] = g.sATT.n
            gpEav.set((g.sATT, g.sATT.n))
            b.op("gpsimd", lambda E, s_=att_sb[:, (i % 2) * 1024:(i % 2 + 1) * 1024],
                 d_=attn_d[i]: E.dma_start(out=d_, in_=s_),
                 waits=[(g.sATT, g.sATT.n)], incs=[(g.sATTD, 16)])
            gatt[i % 2].set((g.sATTD, g.sATTD.n))
            gqt[i % 2].set((g.sS, g.sS.n))
        b.wait("sync", (g.sATTD, g.sATTD.n), (g.sVL, g.sVL.n))

    if DEBUG:
        b.op("sync", lambda E: E.dma_start(out=cw_d, in_=cw_sb[:, :]),
             waits=[(g.sCW, g.sCW.n)], incs=[(g.sDBG, 16)])

    if PHASES >= 3:
        P2_DONE = [(g.sS, g.sS.n), (g.sSC, g.sSC.n), (g.sP, g.sP.n), (g.sPT, g.sPT.n),
                   (g.sPTE, g.sPTE.n), (g.sAV, g.sAV.n), (g.sATT, g.sATT.n),
                   (g.sATTD, g.sATTD.n), (g.sREC, g.sREC.n), (g.sMN, g.sMN.n),
                   (g.sAL, g.sAL.n), (g.sVL, g.sVL.n), (g.sQTL, g.sQTL.n),
                   (g.sKTR, g.sKTR.n)]
        gw13 = Guard()
        gw2 = Guard()
        gxn3s = [Guard(), Guard()]
        XN_MARK = {}
        gy = [Guard(), Guard()]
        gatt3 = [Guard(), Guard()]
        gsil = Guard()
        gyb = {0: Guard(), 1: Guard()}
        nhp = 0

        def emit_xn3_load(gi, extra=()):
            e2, c2 = divmod(gi, 16)
            if e2 >= NE:
                return
            so = (gi % 2) * NDC * 512
            wl = list(extra) + list(gxn3s[gi % 2].need())
            for dc in range(NDC):
                b.op("sync", lambda E, d_=xn3_sb[:, so + dc * 512:so + (dc + 1) * 512],
                     s_=xnT_d[dc, :, c2 * 512:(c2 + 1) * 512]:
                     E.dma_start(out=d_, in_=s_),
                     waits=(wl if dc == 0 else ()), incs=[(g.sXNL, 16)])
            XN_MARK[gi] = g.sXNL.n

        for e in range(NE):
            wt0 = list(gw13.need())
            if e == 0:
                wt0 += P2_DONE + [(g.sWcast, WCAST_FULL)]
            first = True
            for dc in range(NDC):
                b.op("sync", lambda E, d_=w1t_sb[:, dc * H:(dc + 1) * H],
                     s_=w1cm_d[e, dc]: E.dma_start_transpose(d_, s_),
                     waits=(wt0 if first else ()), incs=[(g.sWT, 16)])
                first = False
                b.op("sync", lambda E, d_=w3t_sb[:, dc * H:(dc + 1) * H],
                     s_=w3cm_d[e, dc]: E.dma_start_transpose(d_, s_),
                     incs=[(g.sWT, 16)])
            WT13_E = g.sWT.n
            first = True
            for hc in range(NHC):
                b.op("sync", lambda E, d_=w2t_sb[:, hc * D:(hc + 1) * D],
                     s_=w2cm_d[e, hc]: E.dma_start_transpose(d_, s_),
                     waits=(list(gw2.need()) if first else ()), incs=[(g.sWT, 16)])
                first = False
            WT2_E = g.sWT.n
            if e == 0:
                emit_xn3_load(0, P2_DONE)
                emit_xn3_load(1)
            for c in range(NCH):
                gi = e * NCH + c
                xo = (gi % 2) * NDC * 512
                base_mult = g.sMULT.n
                for hc in range(NHC):
                    nhp += 1
                    wt = [(g.sWT, WT13_E), (g.sXNL, XN_MARK[gi])]
                    if nhp > 1:
                        wt += [(g.sSIL, nhp - 1)]
                    for dc in range(NDC):
                        b.op("tensor", lambda E, l_=w1t_sb[:, dc * H + hc * 128:dc * H + (hc + 1) * 128],
                             r_=xn3_sb[:, xo + dc * 512:xo + (dc + 1) * 512],
                             s0=(dc == 0), s1=(dc == NDC - 1):
                             E.matmul(pA[:], l_, r_, start=s0, stop=s1),
                             waits=(wt if dc == 0 else ()),
                             incs=([(g.sH1, 1)] if dc == NDC - 1 else ()))
                    wt2 = [(g.sMULT, nhp - 1)] if nhp > 1 else []
                    for dc in range(NDC):
                        b.op("tensor", lambda E, l_=w3t_sb[:, dc * H + hc * 128:dc * H + (hc + 1) * 128],
                             r_=xn3_sb[:, xo + dc * 512:xo + (dc + 1) * 512],
                             s0=(dc == 0), s1=(dc == NDC - 1):
                             E.matmul(pC[:], l_, r_, start=s0, stop=s1),
                             waits=(wt2 if dc == 0 else ()),
                             incs=([(g.sH3, 1)] if dc == NDC - 1 else ()))
                    b.op("scalar", lambda E: E.activation(sil_sb[:, :], pA[:], AF.Silu),
                         waits=[(g.sH1, nhp)] + ([(g.sMULT, nhp - 1)] if nhp > 1 else []),
                         incs=[(g.sSIL, 1)])
                    b.op("vector", lambda E, d_=ht_sb[:, hc * 512:(hc + 1) * 512]:
                         E.tensor_tensor(d_, sil_sb[:, :], pC[:], ALU.mult),
                         waits=[(g.sSIL, nhp), (g.sH3, nhp)], incs=[(g.sMULT, 1)])
                for ts in range(4):
                    T = 4 * c + ts
                    pair = ((pE[:, 0:512], pE[:, 512:1024]) if ts % 2 == 0 else (pB, pD))
                    gpair = gyb[ts % 2]
                    wyl = list(gy[ts % 2].need())
                    if e == 0 and c == 0 and ts < 2:
                        wyl += P2_DONE
                    if e == 0:
                        b.op("gpsimd", lambda E, d_=y_sb[:, (ts % 2) * D:(ts % 2 + 1) * D],
                             s_=x_d[T * 128:(T + 1) * 128, :]:
                             E.dma_start(out=d_, in_=s_), waits=wyl, incs=[(g.sYL, 16)])
                        wa3 = list(gatt3[ts % 2].need())
                        if c == 0 and ts < 2:
                            wa3 += P2_DONE
                        b.op("gpsimd", lambda E, d_=att3_sb[:, (ts % 2) * D:(ts % 2 + 1) * D],
                             s_=attn_d[T]: E.dma_start(out=d_, in_=s_),
                             waits=wa3, incs=[(g.sYL, 16)])
                    else:
                        b.op("gpsimd", lambda E, d_=y_sb[:, (ts % 2) * D:(ts % 2 + 1) * D],
                             s_=out_d[T * 128:(T + 1) * 128, :]:
                             E.dma_start(out=d_, in_=s_),
                             waits=wyl + [(g.sYD, 16 * (((e - 1) * NCH + c) * 4 + ts + 1))],
                             incs=[(g.sYL, 16)])
                    for hc in range(NHC):
                        wt = []
                        if hc == 0:
                            wt = list(gpair.need())
                            if c == 0 and ts == 0:
                                wt = wt + [(g.sWT, WT2_E)]
                        if ts == 0:
                            wt = wt + [(g.sMULT, base_mult + hc + 1)]
                        for dh in range(2):
                            b.op("tensor", lambda E, p_=pair[dh],
                                 l_=ht_sb[:, hc * 512 + ts * 128:hc * 512 + ts * 128 + 128],
                                 r_=w2t_sb[:, hc * D + dh * 512:hc * D + (dh + 1) * 512],
                                 s0=(hc == 0), s1=(hc == NHC - 1):
                                 E.matmul(p_, l_, r_, start=s0, stop=s1),
                                 waits=(wt if dh == 0 else ()),
                                 incs=([(g.sYM, 1)] if (hc == NHC - 1 and dh == 1) else ()))
                    cw_col = cw_sb[:, T * 8 + e:T * 8 + e + 1]
                    yv = y_sb[:, (ts % 2) * D:(ts % 2 + 1) * D]
                    wt = [(g.sYM, g.sYM.n), (g.sYL, g.sYL.n)]
                    b.op("vector", lambda E, p_=pair[0], y_=yv[:, 0:512], c_=cw_col:
                         E.scalar_tensor_tensor(y_, p_, c_, y_, ALU.mult, ALU.add),
                         waits=wt)
                    if e == 0:
                        b.op("vector", lambda E, p_=pair[1], y_=yv[:, 512:1024], c_=cw_col:
                             E.scalar_tensor_tensor(y_, p_, c_, y_, ALU.mult, ALU.add))
                        a3 = att3_sb[:, (ts % 2) * D:(ts % 2 + 1) * D]
                        b.op("vector", lambda E, y_=yv, a_=a3:
                             E.tensor_tensor(y_, y_, a_, ALU.add), incs=[(g.sYE, 1)])
                        gatt3[ts % 2].set((g.sYE, g.sYE.n))
                    else:
                        b.op("vector", lambda E, p_=pair[1], y_=yv[:, 512:1024], c_=cw_col:
                             E.scalar_tensor_tensor(y_, p_, c_, y_, ALU.mult, ALU.add),
                             incs=[(g.sYE, 1)])
                    gpair.set((g.sYE, g.sYE.n))
                    b.op("gpsimd", lambda E, s_=yv, d_=out_d[T * 128:(T + 1) * 128, :]:
                         E.dma_start(out=d_, in_=s_), waits=[(g.sYE, g.sYE.n)],
                         incs=[(g.sYD, 16)])
                    gy[ts % 2].set((g.sYD, g.sYD.n))
                gxn3s[gi % 2].set((g.sH3, g.sH3.n))
                emit_xn3_load(gi + 2)
            gw13.set((g.sH3, g.sH3.n))
            gw2.set((g.sYM, g.sYM.n))
        b.wait("sync", (g.sYD, g.sYD.n))

    if PHASES < 2:
        pass_sb = E_(nc.sbuf_tensor("pass_sb", [128, D], F32)).ap()
        for T in range(NT):
            b.op("gpsimd", lambda E, s_=x_d[T * 128:(T + 1) * 128, :], d_=pass_sb:
                 E.dma_start(out=d_, in_=s_),
                 waits=[(g.sPASS, 16 * (2 * T))] if T >= 1 else [],
                 incs=[(g.sPASS, 16)])
            b.op("gpsimd", lambda E, s_=pass_sb, d_=out_d[T * 128:(T + 1) * 128, :]:
                 E.dma_start(out=d_, in_=s_),
                 waits=[(g.sPASS, 16 * (2 * T + 1))], incs=[(g.sPASS, 16)])
        b.wait("sync", (g.sPASS, g.sPASS.n),
               (g.sQTD, g.sQTD.n), (g.sKTD, g.sKTD.n), (g.sVD, g.sVD.n),
               (g.sXNTD, g.sXNTD.n), (g.sXNTLOD, g.sXNTLOD.n),
               (g.sWcast, WCAST_FULL), (g.sCW, g.sCW.n),
               *([(g.sDBG, 16)] if DEBUG else []))

    blk = E_(nc.Block())
    blk.sync(lambda Eng: [fn(Eng) for fn in b.streams["sync"]])
    blk.gpsimd(lambda Eng: [fn(Eng) for fn in b.streams["gpsimd"]])
    blk.tensor(lambda Eng: [fn(Eng) for fn in b.streams["tensor"]])
    blk.vector(lambda Eng: [fn(Eng) for fn in b.streams["vector"]])
    blk.scalar(lambda Eng: [fn(Eng) for fn in b.streams["scalar"]])
    st.close()
    return nc


def kernel(L0_attn_embeddings, L0_attn_W_Q, L0_attn_W_K, L0_attn_W_V,
           L0_ffn_Wg, L0_ffn_W1, L0_ffn_W2, L0_ffn_W3):
    nc = build_nc()
    emb = np.ascontiguousarray(np.asarray(L0_attn_embeddings))
    base = {
        "wq": np.ascontiguousarray(L0_attn_W_Q), "wk": np.ascontiguousarray(L0_attn_W_K),
        "wv": np.ascontiguousarray(L0_attn_W_V), "wg": np.ascontiguousarray(L0_ffn_Wg),
        "w1": np.ascontiguousarray(L0_ffn_W1), "w2": np.ascontiguousarray(L0_ffn_W2),
        "w3": np.ascontiguousarray(L0_ffn_W3),
    }
    in_maps = [dict(base, x=np.ascontiguousarray(emb[i])) for i in range(8)]
    res = run_bass_kernel_spmd(nc, in_maps, core_ids=list(range(8)))
    out = np.stack([np.asarray(r["out"]) for r in res.results], axis=0)
    if DEBUG:
        kernel.debug_results = res.results
    return out


kernel.debug_results = None


# version 12 (final)
# speedup vs baseline: 1.0358x; 1.0358x over previous
import os
import numpy as np
import concourse.bass as bass
import concourse.mybir as mybir
from concourse.bass_utils import run_bass_kernel_spmd
from contextlib import ExitStack

F32 = mybir.dt.float32
BF16 = mybir.dt.bfloat16
AF = mybir.ActivationFunctionType
ALU = mybir.AluOpType
AX = mybir.AxisListType

DEBUG = False
PHASES = 3

S, D, H, NE = 8192, 1024, 2816, 8
NHC = H // 128
NT = S // 128
NCH = S // 512
NDC = D // 128
EPS = 1e-5
NEG = -1e30
CSH = -45.0


class Sem:
    __slots__ = ("h", "n")

    def __init__(self, st, nc, name):
        self.h = st.enter_context(nc.semaphore(name))
        self.n = 0


class Guard:
    __slots__ = ("w",)

    def __init__(self):
        self.w = ()

    def need(self):
        return self.w

    def set(self, *pairs):
        self.w = tuple(pairs)


class B:
    def __init__(self):
        self.streams = {k: [] for k in ("sync", "gpsimd", "tensor", "vector", "scalar")}

    def op(self, eng, fn, waits=(), incs=()):
        for s, v in waits:
            assert v <= s.n, f"[{eng}] wait {v} > emitted {s.n}"
        caps_w = tuple((s.h, int(v)) for s, v in waits if v > 0)
        caps_i = []
        for s, a in incs:
            s.n += a
            caps_i.append((s.h, int(a)))
        caps_i = tuple(caps_i)

        def run(E, _w=caps_w, _f=fn, _i=caps_i):
            for h, v in _w:
                E.wait_ge(h, v)
            ins = _f(E)
            for h, a in _i:
                ins.then_inc(h, a)

        self.streams[eng].append(run)

    def wait(self, eng, *pairs):
        for s, v in pairs:
            assert v <= s.n, f"[{eng}] wait {v} > emitted {s.n}"
        caps = tuple((s.h, int(v)) for s, v in pairs if v > 0)

        def run(E, _w=caps):
            for h, v in _w:
                E.wait_ge(h, v)

        self.streams[eng].append(run)


def build_nc():
    nc = bass.Bass("TRN2", target_bir_lowering=False, debug=False,
                   detect_race_conditions=False)
    io_kind = "ExternalOutput" if DEBUG else "Internal"

    x_d = nc.dram_tensor("x", [S, D], F32, kind="ExternalInput").ap()
    wq_d = nc.dram_tensor("wq", [D, D], F32, kind="ExternalInput").ap()
    wk_d = nc.dram_tensor("wk", [D, D], F32, kind="ExternalInput").ap()
    wv_d = nc.dram_tensor("wv", [D, D], F32, kind="ExternalInput").ap()
    wg_d = nc.dram_tensor("wg", [NE, D], F32, kind="ExternalInput").ap()
    w1_d = nc.dram_tensor("w1", [NE, H, D], F32, kind="ExternalInput").ap()
    w2_d = nc.dram_tensor("w2", [NE, D, H], F32, kind="ExternalInput").ap()
    w3_d = nc.dram_tensor("w3", [NE, H, D], F32, kind="ExternalInput").ap()
    out_d = nc.dram_tensor("out", [S, D], F32, kind="ExternalOutput").ap()

    xnT_d = nc.dram_tensor("xnT_s", [NDC, 128, S], BF16, kind=io_kind).ap()
    xnTlo_d = nc.dram_tensor("xnTlo_s", [NDC, 128, S], BF16, kind=io_kind).ap()
    qT_d = nc.dram_tensor("qT_s", [NDC, 128, S], BF16, kind=io_kind).ap()
    kT_d = nc.dram_tensor("kT_s", [NDC, 128, S], BF16, kind=io_kind).ap()
    v_d = nc.dram_tensor("v_s", [NT, 128, D], BF16, kind=io_kind).ap()
    attn_d = nc.dram_tensor("attn_s", [NT, 128, D], BF16, kind=io_kind).ap()
    cw_d = nc.dram_tensor("cw_s", [128, NT * 8], F32, kind=io_kind).ap()
    sdump_d = nc.dram_tensor("sdump", [NT, 128, 16], F32, kind=io_kind).ap()
    s2dump_d = nc.dram_tensor("s2dump", [NT, 128, 32], F32, kind=io_kind).ap()
    t0dump_d = nc.dram_tensor("t0dump", [128, 1024], F32, kind=io_kind).ap()
    t0pdump_d = nc.dram_tensor("t0pdump", [128, 512], BF16, kind=io_kind).ap()
    t0o_d = nc.dram_tensor("t0o", [128, 1024], F32, kind=io_kind).ap()
    t0pt_d = nc.dram_tensor("t0pt", [128, 256], BF16, kind=io_kind).ap()
    t0v2_d = nc.dram_tensor("t0v2", [128, 4096], BF16, kind=io_kind).ap()
    w1cm_d = nc.dram_tensor("w1cm", [NE, NDC, H, 128], BF16, kind="Internal").ap()
    w3cm_d = nc.dram_tensor("w3cm", [NE, NDC, H, 128], BF16, kind="Internal").ap()
    w2cm_d = nc.dram_tensor("w2cm", [NE, NHC, D, 128], BF16, kind="Internal").ap()

    st = ExitStack()
    E_ = st.enter_context

    id32 = E_(nc.sbuf_tensor("id32", [128, 128], F32)).ap()
    idbf = E_(nc.sbuf_tensor("idbf", [128, 128], BF16)).ap()
    cw_sb = E_(nc.sbuf_tensor("cw_sb", [128, NT * 8], F32)).ap()
    wgT_hi = E_(nc.sbuf_tensor("wgT_hi", [128, NDC * 8], BF16)).ap()
    wgT_lo = E_(nc.sbuf_tensor("wgT_lo", [128, NDC * 8], BF16)).ap()
    stats = E_(nc.sbuf_tensor("stats", [128, 64], F32)).ap()
    tp2 = E_(nc.sbuf_tensor("tp2", [128, 512], F32)).ap()
    wg_sb = E_(nc.sbuf_tensor("wg_sb", [8, D], F32)).ap()

    ARENA = 188928
    arena_h = E_(nc.sbuf_tensor("arena", [128, ARENA], mybir.dt.uint8))
    arena_addr = nc.lookup_mloc(arena_h).addr

    def place(name, shape, dtype, off):
        h = nc.virtual_sbuf_tensor(name, shape, dtype)
        nc._tensor(h.name, list(h.shape), h.dtype, type="SB")
        mloc = nc.lookup_mloc(h)
        mloc.allocated = True
        mloc.addr = arena_addr + off
        return h.ap()

    masks = place("masks", [128, 4 * 512], F32, 180736)
    wq_sb = place("wq_sb", [128, NDC * D], BF16, 0)
    wk_sb = place("wk_sb", [128, NDC * D], BF16, 16384)
    wv_sb = place("wv_sb", [128, NDC * D], BF16, 32768)
    x_sb = place("x_sb", [128, 3 * D], F32, 49152)
    xnf_sb = place("xnf_sb", [128, 3 * D], F32, 61440)
    sqd_sb = place("sqd_sb", [128, D], F32, 73728)
    xnT_sb = place("xnT_sb", [128, 2 * NDC * 512], BF16, 77824)
    xnTlo_sb = place("xnTlo_sb", [128, NDC * 512], BF16, 94208)
    qT_sb = place("qT_sb", [128, 2 * NDC * 512], BF16, 102400)
    kT_sb = place("kT_sb", [128, 2 * NDC * 512], BF16, 118784)
    v_sb = place("v_sb", [128, 2 * 4 * D], BF16, 135168)
    wg_sb2 = None
    lgT_sb = place("lgT_sb", [8, 512], F32, 155648)
    kt_res = place("kt_res", [128, NDC * S], BF16, 0)
    sg_sb = place("sg_sb", [128, 512], F32, 131072)
    v2_sb = place("v2_sb", [128, 3 * 4096], BF16, 133120)
    qt_sb = place("qt_sb", [128, 2 * 1024], BF16, 157696)
    p_sb = place("p_sb", [128, 2 * 512], BF16, 161792)
    pt_sb = place("pt_sb", [128, 3 * 512], BF16, 163840)
    att_sb = place("att_sb", [128, 2 * 1024], BF16, 166912)
    w1t_sb = place("w1t_sb", [128, NDC * H], BF16, 0)
    w3t_sb = place("w3t_sb", [128, NDC * H], BF16, 45056)
    w2t_sb = place("w2t_sb", [128, NHC * D], BF16, 90112)
    ht_sb = place("ht_sb", [128, NHC * 512], BF16, 135168)
    xn3_sb = place("xn3_sb", [128, 2 * NDC * 512], BF16, 157696)
    y_sb = place("y_sb", [128, 2 * D], F32, 174080)
    att3_sb = place("att3_sb", [128, 2 * D], BF16, 182272)
    sil_sb = place("sil_sb", [128, 512], F32, 186368)

    pA = E_(nc.psum_tensor("pA", [128, 512], F32)).ap()
    pB = E_(nc.psum_tensor("pB", [128, 512], F32)).ap()
    pC = E_(nc.psum_tensor("pC", [128, 512], F32)).ap()
    pD = E_(nc.psum_tensor("pD", [128, 512], F32)).ap()
    pE = E_(nc.psum_tensor("pE", [128, 1024], F32)).ap()
    pG = E_(nc.psum_tensor("pG", [128, 512], BF16)).ap()
    pH = E_(nc.psum_tensor("pH", [128, 512], BF16)).ap()

    sems = {}
    for name in ("sSetup", "sWld", "sWgld", "sWgT", "sWgHi", "sWgLo",
                 "sXin", "sVAR", "sRSTD", "sNMR", "sXNF", "sPM", "sLCH",
                 "sTR", "sHI", "sLO", "sLGC", "sLG", "sLGE", "sT2V", "sT2S", "sCW",
                 "sQT", "sQTE", "sQTD", "sKT", "sKTE", "sKTD", "sV", "sVE", "sVD",
                 "sXNTD", "sXNTLOD", "sWcast", "sDBG", "sPASS",
                 "sKTR", "sVL", "sQTL", "sS", "sSC", "sMN", "sAL", "sORS", "sAVB",
                 "sWT", "sXNL", "sYL", "sH1", "sH3", "sSIL", "sMULT", "sYM", "sYE", "sYD",
                 "sT2A", "sT2B", "sRECX", "sREC2",
                 "sP", "sPT", "sPTE", "sAV", "sREC", "sATT", "sATTD"):
        sems[name] = Sem(st, nc, name)
    g = type("G", (), sems)

    b = B()

    gpA, gpB, gpC, gpD = Guard(), Guard(), Guard(), Guard()
    gpE5, gpE6 = Guard(), Guard()
    gpG, gpH = Guard(), Guard()
    gx = [Guard() for _ in range(3)]
    gxnf = [Guard() for _ in range(3)]
    gxnT = [Guard(), Guard()]
    gxnTlo = Guard()
    gqT = [Guard(), Guard()]
    gkT = [Guard(), Guard()]
    gv = [Guard(), Guard()]
    glgT = Guard()

    b.op("gpsimd", lambda E: E.memset(cw_sb[:, :], 0.0))
    b.op("gpsimd", lambda E: E.memset(stats[:, :], 0.0))
    b.op("gpsimd", lambda E: E.memset(stats[:, 9:10], CSH), incs=[(g.sSetup, 1)])
    b.op("gpsimd", lambda E: E.memset(tp2[:, :], 0.0))
    b.op("gpsimd", lambda E: E.memset(id32, 0.0))
    b.op("gpsimd", lambda E: E.affine_select(out=id32, in_=id32, compare_op=ALU.not_equal,
                                             fill=1.0, base=0, pattern=[[-1, 128]],
                                             channel_multiplier=1), incs=[(g.sSetup, 1)])
    b.op("gpsimd", lambda E: E.memset(idbf, 0.0))
    b.op("gpsimd", lambda E: E.affine_select(out=idbf, in_=idbf, compare_op=ALU.not_equal,
                                             fill=1.0, base=0, pattern=[[-1, 128]],
                                             channel_multiplier=1), incs=[(g.sSetup, 1)])
    for p in range(4):
        m = masks[:, p * 512:(p + 1) * 512]
        b.op("gpsimd", lambda E, a=m: E.memset(a, 0.0))
        b.op("gpsimd", lambda E, a=m, pp=p: E.affine_select(
            out=a, in_=a, compare_op=ALU.is_ge, fill=NEG, base=128 * pp,
            pattern=[[-1, 512]], channel_multiplier=1), incs=[(g.sSetup, 1)])

    for w_dram, w_sb in ((wq_d, wq_sb), (wk_d, wk_sb), (wv_d, wv_sb)):
        for dc in range(NDC):
            b.op("gpsimd", lambda E, src=w_dram[dc * 128:(dc + 1) * 128, :],
                 dst=w_sb[:, dc * D:(dc + 1) * D]: E.dma_start(out=dst, in_=src),
                 incs=[(g.sWld, 16)])
    b.op("sync", lambda E: E.dma_start(out=wg_sb, in_=wg_d), incs=[(g.sWgld, 16)])
    WLD_FULL = g.sWld.n

    b.wait("tensor", (g.sWld, WLD_FULL), (g.sWgld, 16), (g.sSetup, 7))
    for dc in range(NDC):
        gd = gpA if dc % 2 == 0 else gpB
        ps = (pA if dc % 2 == 0 else pB)[:, 0:8]
        b.op("tensor", lambda E, p_=ps, src=wg_sb[0:8, dc * 128:(dc + 1) * 128]:
             E.transpose(p_, src, id32[0:8, 0:8]), waits=gd.need(), incs=[(g.sWgT, 1)])
        b.op("scalar", lambda E, p_=ps, d_=wgT_hi[:, dc * 8:(dc + 1) * 8]:
             E.copy(d_, p_), waits=[(g.sWgT, dc + 1)], incs=[(g.sWgHi, 1)])
        b.op("vector", lambda E, p_=ps, h_=wgT_hi[:, dc * 8:(dc + 1) * 8],
             d_=wgT_lo[:, dc * 8:(dc + 1) * 8]:
             E.tensor_tensor(d_, p_, h_, ALU.subtract),
             waits=[(g.sWgHi, dc + 1)], incs=[(g.sWgLo, 1)])
        gd.set((g.sWgLo, dc + 1))

    for e in range(NE):
        for dc in range(NDC):
            b.op("gpsimd", lambda E, d_=w1cm_d[e, dc], s_=w1_d[e, :, dc * 128:(dc + 1) * 128]:
                 E.dma_start(out=d_, in_=s_), incs=[(g.sWcast, 16)])
            b.op("gpsimd", lambda E, d_=w3cm_d[e, dc], s_=w3_d[e, :, dc * 128:(dc + 1) * 128]:
                 E.dma_start(out=d_, in_=s_), incs=[(g.sWcast, 16)])
        for hc in range(NHC):
            b.op("gpsimd", lambda E, d_=w2cm_d[e, hc], s_=w2_d[e, :, hc * 128:(hc + 1) * 128]:
                 E.dma_start(out=d_, in_=s_), incs=[(g.sWcast, 16)])
    WCAST_FULL = g.sWcast.n

    ntr = 0
    tile_tr_end = {}

    def tr_slot(k):
        return (gpA, pA) if k % 2 == 0 else (gpB, pB)

    def p1_tile(T):
        nonlocal ntr
        c, tl = divmod(T, 4)
        xs = x_sb[:, (T % 3) * D:(T % 3 + 1) * D]
        xn = xnf_sb[:, (T % 3) * D:(T % 3 + 1) * D]
        b.op("sync", lambda E, d_=xs, s_=x_d[T * 128:(T + 1) * 128, :]:
             E.dma_start(out=d_, in_=s_), waits=gx[T % 3].need(), incs=[(g.sXin, 16)])
        b.wait("vector", (g.sXin, (T + 1) * 16))
        b.op("vector", lambda E, s_=xs: E.reduce_sum(stats[:, 0:1], s_, axis=AX.X))
        b.op("vector", lambda E, s_=xs: E.scalar_tensor_tensor(
            sqd_sb[:, :], s_, 1.0, s_, ALU.mult, ALU.mult, accum_out=stats[:, 1:2]))
        b.op("vector", lambda E: E.tensor_scalar_mul(stats[:, 2:3], stats[:, 0:1], 1.0 / D))
        b.op("vector", lambda E: E.tensor_scalar_mul(stats[:, 3:4], stats[:, 1:2], 1.0 / D))
        b.op("vector", lambda E: E.tensor_mul(stats[:, 4:5], stats[:, 2:3], stats[:, 2:3]))
        b.op("vector", lambda E, s_=xs, d_=xn: E.tensor_scalar_sub(d_, s_, stats[:, 2:3]),
             waits=gxnf[T % 3].need())
        b.op("vector", lambda E: E.tensor_scalar(stats[:, 5:6], stats[:, 3:4],
                                                 stats[:, 4:5], EPS, ALU.subtract, ALU.add),
             incs=[(g.sVAR, 1)])
        b.wait("scalar", (g.sVAR, T + 1))
        b.op("scalar", lambda E: E.activation(stats[:, 6:7], stats[:, 5:6], AF.Ln))
        b.op("scalar", lambda E: E.activation(stats[:, 7:8], stats[:, 6:7], AF.Exp, scale=-0.5),
             incs=[(g.sRSTD, 1)])
        if DEBUG:
            b.op("sync", lambda E, d_=sdump_d[T]: E.dma_start(out=d_, in_=stats[:, 0:16]),
                 waits=[(g.sRSTD, T + 1)], incs=[(g.sDBG, 16)])
        b.wait("vector", (g.sRSTD, T + 1))
        b.op("vector", lambda E, d_=xn: E.tensor_scalar_mul(
            d_, d_, stats[:, 7:8]), incs=[(g.sXNF, 1)])
        gx[T % 3].set((g.sXNF, g.sXNF.n))
        b.wait("tensor", (g.sXNF, T + 1))
        for dc in range(NDC):
            gd, pp = tr_slot(ntr)
            ps = pp[:, 0:128]
            wt = list(gd.need())
            if tl == 0 and dc == 0:
                wt += list(gxnT[c % 2].need())
                if c >= 1:
                    wt += list(gxnTlo.need())
            b.op("tensor", lambda E, p_=ps, s_=xn[:, dc * 128:(dc + 1) * 128]:
                 E.transpose(p_, s_, id32), waits=wt, incs=[(g.sTR, 1)])
            hoff = (c % 2) * NDC * 512 + dc * 512 + tl * 128
            hi = xnT_sb[:, hoff:hoff + 128]
            lo = xnTlo_sb[:, dc * 512 + tl * 128: dc * 512 + tl * 128 + 128]
            b.op("scalar", lambda E, p_=ps, d_=hi: E.copy(d_, p_),
                 waits=[(g.sTR, ntr + 1)], incs=[(g.sHI, 1)])
            b.op("vector", lambda E, p_=ps, h_=hi, d_=lo:
                 E.tensor_tensor(d_, p_, h_, ALU.subtract),
                 waits=[(g.sHI, g.sHI.n)], incs=[(g.sLO, 1)])
            gd.set((g.sHI, g.sHI.n), (g.sLO, g.sLO.n))
            ntr += 1
        tile_tr_end[T] = g.sTR.n
        gxnf[T % 3].set((g.sTR, g.sTR.n))

    def p1_chunk(c):
        nonlocal ntr
        hi0 = (c % 2) * NDC * 512

        def xh(dc):
            return xnT_sb[:, hi0 + dc * 512: hi0 + (dc + 1) * 512]

        def xl(dc):
            return xnTlo_sb[:, dc * 512:(dc + 1) * 512]

        terms = []
        for dc in range(NDC):
            terms.append((wgT_hi[:, dc * 8:(dc + 1) * 8], xh(dc)))
            terms.append((wgT_hi[:, dc * 8:(dc + 1) * 8], xl(dc)))
            terms.append((wgT_lo[:, dc * 8:(dc + 1) * 8], xh(dc)))
        wt0 = [(g.sHI, g.sHI.n), (g.sLO, g.sLO.n)] + list(gpC.need())
        for i, (lh, rh) in enumerate(terms):
            b.op("tensor", lambda E, l_=lh, r_=rh, s0=(i == 0), s1=(i == len(terms) - 1):
                 E.matmul(pC[0:8, :], l_, r_, start=s0, stop=s1),
                 waits=(wt0 if i == 0 else ()),
                 incs=([(g.sLG, 1)] if i == len(terms) - 1 else ()))
        b.op("scalar", lambda E: E.copy(lgT_sb[:, :], pC[0:8, :]),
             waits=[(g.sLG, c + 1)] + list(glgT.need()), incs=[(g.sLGE, 1)])
        gpC.set((g.sLGE, g.sLGE.n))

        LG, P8, MK1, MSD, MK2 = 0, 32, 64, 96, 128
        NM1, M2C, ZC, DEN, REC, REC2 = 160, 164, 168, 172, 176, 180
        for tl in range(4):
            gd, pp = tr_slot(ntr)
            ps = pp[:, 0:8]
            b.op("tensor", lambda E, p_=ps, s_=lgT_sb[0:8, tl * 128:(tl + 1) * 128]:
                 E.transpose(p_, s_, id32[0:8, 0:8]),
                 waits=[(g.sLGE, c + 1)] + list(gd.need()), incs=[(g.sTR, 1)])
            b.op("vector", lambda E, p_=ps, d_=tp2[:, LG + tl * 8:LG + (tl + 1) * 8]:
                 E.tensor_copy(d_, p_),
                 waits=[(g.sTR, ntr + 1)], incs=[(g.sLGC, 1)])
            gd.set((g.sLGC, g.sLGC.n))
            ntr += 1
        for tl in range(4):
            b.op("vector", lambda E, i=tl: E.reduce_max(
                tp2[:, NM1 + i:NM1 + i + 1], tp2[:, LG + i * 8:LG + (i + 1) * 8],
                axis=AX.X, negate=True), incs=([(g.sT2A, 1)] if tl == 3 else ()))
        b.wait("scalar", (g.sT2A, g.sT2A.n))
        for tl in range(4):
            b.op("scalar", lambda E, i=tl: E.activation(
                tp2[:, P8 + i * 8:P8 + (i + 1) * 8], tp2[:, LG + i * 8:LG + (i + 1) * 8],
                AF.Exp, bias=tp2[:, NM1 + i:NM1 + i + 1]),
                incs=([(g.sT2S, 1)] if tl == 3 else ()))
        b.wait("vector", (g.sT2S, g.sT2S.n))
        for tl in range(4):
            b.op("vector", lambda E, i=tl: E.tensor_scalar(
                tp2[:, MK1 + i * 8:MK1 + (i + 1) * 8], tp2[:, P8 + i * 8:P8 + (i + 1) * 8],
                1.0, 0.0, ALU.subtract, ALU.is_ge))
        b.op("vector", lambda E: E.tensor_copy(tp2[:, 256:512], masks[:, 0:256]))
        for tl in range(4):
            b.op("vector", lambda E, i=tl: E.scalar_tensor_tensor(
                tp2[:, MSD + i * 8:MSD + (i + 1) * 8], tp2[:, MK1 + i * 8:MK1 + (i + 1) * 8],
                NEG, tp2[:, LG + i * 8:LG + (i + 1) * 8], ALU.mult, ALU.add))
        b.op("vector", lambda E: E.tensor_copy(tp2[:, 256:512], masks[:, 0:256]))
        for tl in range(4):
            b.op("vector", lambda E, i=tl: E.reduce_max(
                tp2[:, M2C + i:M2C + i + 1], tp2[:, MSD + i * 8:MSD + (i + 1) * 8],
                axis=AX.X), incs=([(g.sT2B, 1)] if tl == 3 else ()))
        b.wait("scalar", (g.sT2B, g.sT2B.n))
        for tl in range(4):
            b.op("scalar", lambda E, i=tl: E.activation(
                tp2[:, ZC + i:ZC + i + 1], tp2[:, M2C + i:M2C + i + 1], AF.Exp,
                bias=tp2[:, NM1 + i:NM1 + i + 1]),
                incs=([(g.sT2S, 1)] if tl == 3 else ()))
        b.wait("vector", (g.sT2S, g.sT2S.n))
        for tl in range(4):
            b.op("vector", lambda E, i=tl: E.tensor_scalar_add(
                tp2[:, DEN + i:DEN + i + 1], tp2[:, ZC + i:ZC + i + 1], 1.0))
        for tl in range(4):
            b.op("vector", lambda E, i=tl: E.tensor_scalar(
                tp2[:, MK2 + i * 8:MK2 + (i + 1) * 8], tp2[:, P8 + i * 8:P8 + (i + 1) * 8],
                tp2[:, ZC + i:ZC + i + 1], 0.0, ALU.subtract, ALU.is_ge))
        for tl in range(4):
            b.op("vector", lambda E, i=tl: E.reciprocal(
                tp2[:, REC + i:REC + i + 1], tp2[:, DEN + i:DEN + i + 1]),
                incs=([(g.sRECX, 1)] if tl == 3 else ()))
        b.wait("scalar", (g.sRECX, g.sRECX.n))
        for tl in range(4):
            b.op("scalar", lambda E, i=tl: E.copy(
                tp2[:, REC2 + i:REC2 + i + 1], tp2[:, REC + i:REC + i + 1]),
                incs=([(g.sREC2, 1)] if tl == 3 else ()))
        b.wait("vector", (g.sREC2, g.sREC2.n))
        for tl in range(4):
            T = 4 * c + tl
            b.op("vector", lambda E, i=tl, d_=cw_sb[:, T * 8:(T + 1) * 8]:
                 E.scalar_tensor_tensor(d_, tp2[:, P8 + i * 8:P8 + (i + 1) * 8],
                                        tp2[:, REC2 + i:REC2 + i + 1],
                                        tp2[:, MK2 + i * 8:MK2 + (i + 1) * 8],
                                        ALU.mult, ALU.mult), incs=[(g.sCW, 1)])
        glgT.set((g.sLGC, g.sLGC.n))

        for (w_sb, sMM, sME, sMD, dram, dst_sb, gslot, scale) in (
                (wq_sb, g.sQT, g.sQTE, g.sQTD, qT_d, qT_sb, gqT, 1.0 / 32.0),
                (wk_sb, g.sKT, g.sKTE, g.sKTD, kT_d, kT_sb, gkT, None)):
            for do in range(NDC):
                gd = gpC if do % 2 == 0 else gpD
                ps = pC if do % 2 == 0 else pD
                for di in range(NDC):
                    lh = w_sb[:, di * D + do * 128: di * D + (do + 1) * 128]
                    b.op("tensor", lambda E, p_=ps, l_=lh, r_=xh(di),
                         s0=(di == 0), s1=(di == NDC - 1):
                         E.matmul(p_, l_, r_, start=s0, stop=s1),
                         waits=(gd.need() if di == 0 else ()),
                         incs=([(sMM, 1)] if di == NDC - 1 else ()))
                dsoff = (c % 2) * NDC * 512 + do * 512
                dst = dst_sb[:, dsoff:dsoff + 512]
                wsl = list(gslot[c % 2].need()) if do == 0 else []
                if scale is None:
                    b.op("scalar", lambda E, p_=ps, d_=dst: E.copy(d_, p_),
                         waits=[(sMM, sMM.n)] + wsl, incs=[(sME, 1)])
                else:
                    b.op("scalar", lambda E, p_=ps, d_=dst, sc_=scale: E.activation(
                        d_, p_, AF.Copy, scale=sc_), waits=[(sMM, sMM.n)] + wsl,
                        incs=[(sME, 1)])
                gd.set((sME, sME.n))
                b.op("sync", lambda E, s_=dst, d_=dram[do, :, c * 512:(c + 1) * 512]:
                     E.dma_start(out=d_, in_=s_), waits=[(sME, sME.n)], incs=[(sMD, 16)])
            gslot[c % 2].set((sMD, sMD.n))

        for ts in range(4):
            T = 4 * c + ts
            pairs = (((gpE5, pE[:, 0:512]), (gpE6, pE[:, 512:1024])) if ts % 2 == 0
                     else ((gpC, pC), (gpD, pD)))
            for dh in range(2):
                gd, ps = pairs[dh]
                for di in range(NDC):
                    lh = xnT_sb[:, hi0 + di * 512 + ts * 128: hi0 + di * 512 + ts * 128 + 128]
                    rh = wv_sb[:, di * D + dh * 512: di * D + (dh + 1) * 512]
                    b.op("tensor", lambda E, p_=ps, l_=lh, r_=rh,
                         s0=(di == 0), s1=(di == NDC - 1):
                         E.matmul(p_, l_, r_, start=s0, stop=s1),
                         waits=(gd.need() if di == 0 else ()),
                         incs=([(g.sV, 1)] if di == NDC - 1 else ()))
                voff = (c % 2) * 4 * D + ts * D + dh * 512
                dst = v_sb[:, voff:voff + 512]
                wsl = list(gv[c % 2].need()) if (ts == 0 and dh == 0) else []
                b.op("scalar", lambda E, p_=ps, d_=dst: E.copy(d_, p_),
                     waits=[(g.sV, g.sV.n)] + wsl, incs=[(g.sVE, 1)])
                gd.set((g.sVE, g.sVE.n))
            src = v_sb[:, (c % 2) * 4 * D + ts * D:(c % 2) * 4 * D + (ts + 1) * D]
            b.op("sync", lambda E, s_=src, d_=v_d[T]: E.dma_start(out=d_, in_=s_),
                 waits=[(g.sVE, g.sVE.n)], incs=[(g.sVD, 16)])
        gv[c % 2].set((g.sVD, g.sVD.n))

        for dc in range(NDC):
            b.op("sync", lambda E, s_=xh(dc), d_=xnT_d[dc, :, c * 512:(c + 1) * 512]:
                 E.dma_start(out=d_, in_=s_), waits=[(g.sHI, g.sHI.n)],
                 incs=[(g.sXNTD, 16)])
            b.op("sync", lambda E, s_=xl(dc), d_=xnTlo_d[dc, :, c * 512:(c + 1) * 512]:
                 E.dma_start(out=d_, in_=s_), waits=[(g.sLO, g.sLO.n)],
                 incs=[(g.sXNTLOD, 16)])
        gxnT[c % 2].set((g.sV, g.sV.n), (g.sXNTD, g.sXNTD.n), (g.sLG, g.sLG.n))
        gxnTlo.set((g.sLG, g.sLG.n), (g.sXNTLOD, g.sXNTLOD.n))

    b.wait("tensor", (g.sWgLo, 8), (g.sSetup, 7))
    for c in range(NCH):
        for tl in range(4):
            p1_tile(4 * c + tl)
        p1_chunk(c)

    if PHASES >= 2:
        P1_DONE = [(g.sV, g.sV.n), (g.sXNF, g.sXNF.n), (g.sLG, g.sLG.n),
                   (g.sQT, g.sQT.n), (g.sKT, g.sKT.n), (g.sQTE, g.sQTE.n),
                   (g.sKTE, g.sKTE.n), (g.sVE, g.sVE.n), (g.sHI, g.sHI.n),
                   (g.sLO, g.sLO.n), (g.sLGC, g.sLGC.n), (g.sCW, g.sCW.n),
                   (g.sQTD, g.sQTD.n), (g.sKTD, g.sKTD.n), (g.sVD, g.sVD.n),
                   (g.sXNTD, g.sXNTD.n), (g.sXNTLOD, g.sXNTLOD.n)]
        for dc in range(NDC):
            b.op("sync", lambda E, d_=kt_res[:, dc * S:(dc + 1) * S], s_=kT_d[dc]:
                 E.dma_start(out=d_, in_=s_),
                 waits=(P1_DONE if dc == 0 else ()), incs=[(g.sKTR, 16)])
        b.wait("gpsimd", *P1_DONE)

        gqt = [Guard(), Guard()]
        gv2 = [Guard(), Guard(), Guard()]
        gps2 = [Guard(), Guard()]
        gpt = [Guard(), Guard(), Guard()]
        gatt = [Guard(), Guard()]
        gsg = Guard()
        gpEav = Guard()
        pte_mark = {}
        vl_mark = {}
        rec_mark = {}
        att_mark = {}

        def emit_s(i, ck, Ci, QTL_I, lbase):
            gd, pp = (gpA, pA) if ck % 2 == 0 else (gpB, pB)
            wt = list(gd.need())
            if ck == 0:
                wt += [(g.sQTL, QTL_I), (g.sKTR, 8 * 16)]
            for dc in range(NDC):
                lh = qt_sb[:, (i % 2) * 1024 + dc * 128:(i % 2) * 1024 + (dc + 1) * 128]
                rh = kt_res[:, dc * S + ck * 512: dc * S + (ck + 1) * 512]
                b.op("tensor", lambda E, p_=pp, l_=lh, r_=rh,
                     s0=(dc == 0), s1=(dc == NDC - 1):
                     E.matmul(p_, l_, r_, start=s0, stop=s1),
                     waits=(wt if dc == 0 else ()),
                     incs=([(g.sS, 1)] if dc == NDC - 1 else ()))
            b.op("sync", lambda E, d_=v2_sb[:, (ck % 3) * 4096:(ck % 3 + 1) * 4096],
                 s_=v_d[4 * ck:4 * ck + 4].rearrange("j p d -> p j d"):
                 E.dma_start(out=d_, in_=s_), waits=list(gv2[ck % 3].need()),
                 incs=[(g.sVL, 16)])
            vl_mark[ck] = g.sVL.n
            lcol = stats[:, lbase + ck:lbase + ck + 1]
            wtE = list(gps2[ck % 2].need())
            if i >= 2:
                wtE += [(g.sREC, rec_mark[i - 2])]
            if ck == Ci - 1:
                mp = masks[:, (i % 4) * 512:(i % 4 + 1) * 512]
                b.op("vector", lambda E, p_=pp, d_=sg_sb[:, 0:512], m_=mp:
                     E.tensor_tensor(d_, p_, m_, ALU.add),
                     waits=[(g.sS, g.sS.n)] + list(gsg.need()), incs=[(g.sSC, 1)])
                b.op("scalar", lambda E, d_=p_sb[:, (ck % 2) * 512:(ck % 2 + 1) * 512],
                     a_=lcol: E.activation(d_, sg_sb[:, 0:512], AF.Exp,
                                           bias=stats[:, 9:10], accum_out=a_),
                     waits=[(g.sSC, g.sSC.n)] + wtE, incs=[(g.sP, 1)])
                gd.set((g.sSC, g.sSC.n))
                gsg.set((g.sP, g.sP.n))
            else:
                b.op("scalar", lambda E, p_=pp,
                     d_=p_sb[:, (ck % 2) * 512:(ck % 2 + 1) * 512],
                     a_=lcol: E.activation(d_, p_, AF.Exp, bias=stats[:, 9:10],
                                           accum_out=a_),
                     waits=[(g.sS, g.sS.n)] + wtE, incs=[(g.sP, 1)])
                gd.set((g.sP, g.sP.n))

        def emit_t(i, ck):
            gdq, pq = (gpG, pG) if ck % 2 == 0 else (gpH, pH)
            for blk in range(4):
                wtT = [(g.sP, g.sP.n)] + (list(gdq.need()) if blk == 0 else [])
                b.op("tensor", lambda E, p_=pq[:, blk * 128:(blk + 1) * 128],
                     s_=p_sb[:, (ck % 2) * 512 + blk * 128:(ck % 2) * 512 + (blk + 1) * 128]:
                     E.transpose(p_, s_, idbf), waits=wtT,
                     incs=([(g.sPT, 1)] if blk == 3 else ()))
            b.op("vector", lambda E, p_=pq[:, 0:512],
                 d_=pt_sb[:, (ck % 3) * 512:(ck % 3 + 1) * 512]:
                 E.tensor_copy(d_, p_),
                 waits=[(g.sPT, g.sPT.n)] + list(gpt[ck % 3].need()),
                 incs=[(g.sPTE, 1)])
            pte_mark[ck] = g.sPTE.n
            gdq.set((g.sPTE, g.sPTE.n))
            gps2[ck % 2].set((g.sPT, g.sPT.n))

        def emit_av(i, ck, Ci):
            slot = ck % 3
            first = (ck == 0)
            last = (ck == Ci - 1)
            pm, vm = pte_mark[ck], vl_mark[ck]
            for blk in range(4):
                for dh in range(2):
                    wt2 = ([(g.sPTE, pm), (g.sVL, vm)]
                           if (blk == 0 and dh == 0) else [])
                    if first and blk == 0:
                        wt2 = wt2 + list(gpEav.need())
                    b.op("tensor", lambda E,
                         p_=(pE[:, 0:512] if dh == 0 else pE[:, 512:1024]),
                         l_=pt_sb[:, slot * 512 + blk * 128:
                                  slot * 512 + (blk + 1) * 128],
                         r_=v2_sb[:, slot * 4096 + blk * 1024 + dh * 512:
                                  slot * 4096 + blk * 1024 + (dh + 1) * 512],
                         s0=(first and blk == 0), s1=(last and blk == 3):
                         E.matmul(p_, l_, r_, start=s0, stop=s1),
                         waits=wt2,
                         incs=([(g.sAV, 1)] if dh == 1 else []))
            gpt[slot].set((g.sAV, g.sAV.n))
            gv2[slot].set((g.sAV, g.sAV.n))

        for i in range(min(NT, NCH * 4)):
            Ci = i // 4 + 1
            lbase = 32 + (i % 2) * 16
            b.op("sync", lambda E, d_=qt_sb[:, (i % 2) * 1024:(i % 2 + 1) * 1024],
                 s_=qT_d[:, :, i * 128:(i + 1) * 128].rearrange("c p t -> p c t"):
                 E.dma_start(out=d_, in_=s_), waits=list(gqt[i % 2].need()),
                 incs=[(g.sQTL, 16)])
            QTL_I = g.sQTL.n
            for ck in range(Ci):
                emit_s(i, ck, Ci, QTL_I, lbase)
                if ck >= 2:
                    emit_av(i, ck - 2, Ci)
                if ck >= 1:
                    emit_t(i, ck - 1)
            if Ci >= 2:
                emit_av(i, Ci - 2, Ci)
            emit_t(i, Ci - 1)
            emit_av(i, Ci - 1, Ci)
            b.op("scalar", lambda E: E.copy(stats[:, 28:29], stats[:, 9:10]),
                 incs=[(g.sPM, 1)])
            b.op("vector", lambda E, n_=Ci, o_=lbase: E.reduce_sum(
                stats[:, 27:28], stats[:, o_:o_ + n_], axis=AX.X),
                waits=[(g.sP, g.sP.n), (g.sPM, i + 1)], incs=[(g.sLCH, 1)])
            b.op("vector", lambda E: E.tensor_copy(tp2[:, 256:512], masks[:, 0:256]))
            wtR = [(g.sATT, att_mark[i - 2])] if i >= 2 else []
            b.op("vector", lambda E, c_=30 + (i % 2): E.reciprocal(
                stats[:, c_:c_ + 1], stats[:, 27:28]),
                waits=wtR + [(g.sLCH, i + 1)])
            b.op("vector", lambda E: E.tensor_copy(tp2[:, 256:512], masks[:, 0:256]),
                 incs=[(g.sREC, 1)])
            rec_mark[i] = g.sREC.n
            wt = [(g.sREC, g.sREC.n), (g.sAV, g.sAV.n)] + list(gatt[i % 2].need())
            b.op("scalar", lambda E, d_=att_sb[:, (i % 2) * 1024:(i % 2 + 1) * 1024],
                 c_=30 + (i % 2):
                 E.activation(d_, pE[:, :], AF.Copy, scale=stats[:, c_:c_ + 1]),
                 waits=wt, incs=[(g.sATT, 1)])
            att_mark[i] = g.sATT.n
            gpEav.set((g.sATT, g.sATT.n))
            b.op("gpsimd", lambda E, s_=att_sb[:, (i % 2) * 1024:(i % 2 + 1) * 1024],
                 d_=attn_d[i]: E.dma_start(out=d_, in_=s_),
                 waits=[(g.sATT, g.sATT.n)], incs=[(g.sATTD, 16)])
            gatt[i % 2].set((g.sATTD, g.sATTD.n))
            gqt[i % 2].set((g.sS, g.sS.n))
        b.wait("sync", (g.sATTD, g.sATTD.n), (g.sVL, g.sVL.n))

    if DEBUG:
        b.op("sync", lambda E: E.dma_start(out=cw_d, in_=cw_sb[:, :]),
             waits=[(g.sCW, g.sCW.n)], incs=[(g.sDBG, 16)])

    if PHASES >= 3:
        P2_DONE = [(g.sS, g.sS.n), (g.sSC, g.sSC.n), (g.sP, g.sP.n), (g.sPT, g.sPT.n),
                   (g.sPTE, g.sPTE.n), (g.sAV, g.sAV.n), (g.sATT, g.sATT.n),
                   (g.sATTD, g.sATTD.n), (g.sREC, g.sREC.n), (g.sMN, g.sMN.n),
                   (g.sAL, g.sAL.n), (g.sVL, g.sVL.n), (g.sQTL, g.sQTL.n),
                   (g.sKTR, g.sKTR.n)]
        gw13 = Guard()
        gw2 = Guard()
        gxn3s = [Guard(), Guard()]
        XN_MARK = {}
        gy = [Guard(), Guard()]
        gatt3 = [Guard(), Guard()]
        gsil = Guard()
        gyb = {0: Guard(), 1: Guard()}
        nhp = 0

        def emit_xn3_load(gi, extra=()):
            e2, c2 = divmod(gi, 16)
            if e2 >= NE:
                return
            so = (gi % 2) * NDC * 512
            wl = list(extra) + list(gxn3s[gi % 2].need())
            for dc in range(NDC):
                b.op("sync", lambda E, d_=xn3_sb[:, so + dc * 512:so + (dc + 1) * 512],
                     s_=xnT_d[dc, :, c2 * 512:(c2 + 1) * 512]:
                     E.dma_start(out=d_, in_=s_),
                     waits=(wl if dc == 0 else ()), incs=[(g.sXNL, 16)])
            XN_MARK[gi] = g.sXNL.n

        for e in range(NE):
            wt0 = list(gw13.need())
            if e == 0:
                wt0 += P2_DONE + [(g.sWcast, WCAST_FULL)]
            first = True
            for dc in range(NDC):
                b.op("sync", lambda E, d_=w1t_sb[:, dc * H:(dc + 1) * H],
                     s_=w1cm_d[e, dc]: E.dma_start_transpose(d_, s_),
                     waits=(wt0 if first else ()), incs=[(g.sWT, 16)])
                first = False
                b.op("sync", lambda E, d_=w3t_sb[:, dc * H:(dc + 1) * H],
                     s_=w3cm_d[e, dc]: E.dma_start_transpose(d_, s_),
                     incs=[(g.sWT, 16)])
            WT13_E = g.sWT.n
            first = True
            for hc in range(NHC):
                b.op("sync", lambda E, d_=w2t_sb[:, hc * D:(hc + 1) * D],
                     s_=w2cm_d[e, hc]: E.dma_start_transpose(d_, s_),
                     waits=(list(gw2.need()) if first else ()), incs=[(g.sWT, 16)])
                first = False
            WT2_E = g.sWT.n
            if e == 0:
                emit_xn3_load(0, P2_DONE)
                emit_xn3_load(1)
            for c in range(NCH):
                gi = e * NCH + c
                xo = (gi % 2) * NDC * 512
                base_mult = g.sMULT.n
                for hc in range(NHC):
                    nhp += 1
                    wt = [(g.sWT, WT13_E), (g.sXNL, XN_MARK[gi])]
                    if nhp > 1:
                        wt += [(g.sSIL, nhp - 1)]
                    for dc in range(NDC):
                        b.op("tensor", lambda E, l_=w1t_sb[:, dc * H + hc * 128:dc * H + (hc + 1) * 128],
                             r_=xn3_sb[:, xo + dc * 512:xo + (dc + 1) * 512],
                             s0=(dc == 0), s1=(dc == NDC - 1):
                             E.matmul(pA[:], l_, r_, start=s0, stop=s1),
                             waits=(wt if dc == 0 else ()),
                             incs=([(g.sH1, 1)] if dc == NDC - 1 else ()))
                    wt2 = [(g.sMULT, nhp - 1)] if nhp > 1 else []
                    for dc in range(NDC):
                        b.op("tensor", lambda E, l_=w3t_sb[:, dc * H + hc * 128:dc * H + (hc + 1) * 128],
                             r_=xn3_sb[:, xo + dc * 512:xo + (dc + 1) * 512],
                             s0=(dc == 0), s1=(dc == NDC - 1):
                             E.matmul(pC[:], l_, r_, start=s0, stop=s1),
                             waits=(wt2 if dc == 0 else ()),
                             incs=([(g.sH3, 1)] if dc == NDC - 1 else ()))
                    b.op("scalar", lambda E: E.activation(sil_sb[:, :], pA[:], AF.Silu),
                         waits=[(g.sH1, nhp)] + ([(g.sMULT, nhp - 1)] if nhp > 1 else []),
                         incs=[(g.sSIL, 1)])
                    b.op("vector", lambda E, d_=ht_sb[:, hc * 512:(hc + 1) * 512]:
                         E.tensor_tensor(d_, sil_sb[:, :], pC[:], ALU.mult),
                         waits=[(g.sSIL, nhp), (g.sH3, nhp)], incs=[(g.sMULT, 1)])
                for ts in range(4):
                    T = 4 * c + ts
                    pair = ((pE[:, 0:512], pE[:, 512:1024]) if ts % 2 == 0 else (pB, pD))
                    gpair = gyb[ts % 2]
                    wyl = list(gy[ts % 2].need())
                    if e == 0 and c == 0 and ts < 2:
                        wyl += P2_DONE
                    if e == 0:
                        b.op("gpsimd", lambda E, d_=y_sb[:, (ts % 2) * D:(ts % 2 + 1) * D],
                             s_=x_d[T * 128:(T + 1) * 128, :]:
                             E.dma_start(out=d_, in_=s_), waits=wyl, incs=[(g.sYL, 16)])
                        wa3 = list(gatt3[ts % 2].need())
                        if c == 0 and ts < 2:
                            wa3 += P2_DONE
                        b.op("gpsimd", lambda E, d_=att3_sb[:, (ts % 2) * D:(ts % 2 + 1) * D],
                             s_=attn_d[T]: E.dma_start(out=d_, in_=s_),
                             waits=wa3, incs=[(g.sYL, 16)])
                    else:
                        b.op("gpsimd", lambda E, d_=y_sb[:, (ts % 2) * D:(ts % 2 + 1) * D],
                             s_=out_d[T * 128:(T + 1) * 128, :]:
                             E.dma_start(out=d_, in_=s_),
                             waits=wyl + [(g.sYD, 16 * (((e - 1) * NCH + c) * 4 + ts + 1))],
                             incs=[(g.sYL, 16)])
                    for hc in range(NHC):
                        wt = []
                        if hc == 0:
                            wt = list(gpair.need())
                            if c == 0 and ts == 0:
                                wt = wt + [(g.sWT, WT2_E)]
                        if ts == 0:
                            wt = wt + [(g.sMULT, base_mult + hc + 1)]
                        for dh in range(2):
                            b.op("tensor", lambda E, p_=pair[dh],
                                 l_=ht_sb[:, hc * 512 + ts * 128:hc * 512 + ts * 128 + 128],
                                 r_=w2t_sb[:, hc * D + dh * 512:hc * D + (dh + 1) * 512],
                                 s0=(hc == 0), s1=(hc == NHC - 1):
                                 E.matmul(p_, l_, r_, start=s0, stop=s1),
                                 waits=(wt if dh == 0 else ()),
                                 incs=([(g.sYM, 1)] if (hc == NHC - 1 and dh == 1) else ()))
                    cw_col = cw_sb[:, T * 8 + e:T * 8 + e + 1]
                    yv = y_sb[:, (ts % 2) * D:(ts % 2 + 1) * D]
                    wt = [(g.sYM, g.sYM.n), (g.sYL, g.sYL.n)]
                    b.op("vector", lambda E, p_=pair[0], y_=yv[:, 0:512], c_=cw_col:
                         E.scalar_tensor_tensor(y_, p_, c_, y_, ALU.mult, ALU.add),
                         waits=wt)
                    if e == 0:
                        b.op("vector", lambda E, p_=pair[1], y_=yv[:, 512:1024], c_=cw_col:
                             E.scalar_tensor_tensor(y_, p_, c_, y_, ALU.mult, ALU.add))
                        a3 = att3_sb[:, (ts % 2) * D:(ts % 2 + 1) * D]
                        b.op("vector", lambda E, y_=yv, a_=a3:
                             E.tensor_tensor(y_, y_, a_, ALU.add), incs=[(g.sYE, 1)])
                        gatt3[ts % 2].set((g.sYE, g.sYE.n))
                    else:
                        b.op("vector", lambda E, p_=pair[1], y_=yv[:, 512:1024], c_=cw_col:
                             E.scalar_tensor_tensor(y_, p_, c_, y_, ALU.mult, ALU.add),
                             incs=[(g.sYE, 1)])
                    gpair.set((g.sYE, g.sYE.n))
                    b.op("gpsimd", lambda E, s_=yv, d_=out_d[T * 128:(T + 1) * 128, :]:
                         E.dma_start(out=d_, in_=s_), waits=[(g.sYE, g.sYE.n)],
                         incs=[(g.sYD, 16)])
                    gy[ts % 2].set((g.sYD, g.sYD.n))
                gxn3s[gi % 2].set((g.sH3, g.sH3.n))
                emit_xn3_load(gi + 2)
            gw13.set((g.sH3, g.sH3.n))
            gw2.set((g.sYM, g.sYM.n))
        b.wait("sync", (g.sYD, g.sYD.n))

    if PHASES < 2:
        pass_sb = E_(nc.sbuf_tensor("pass_sb", [128, D], F32)).ap()
        for T in range(NT):
            b.op("gpsimd", lambda E, s_=x_d[T * 128:(T + 1) * 128, :], d_=pass_sb:
                 E.dma_start(out=d_, in_=s_),
                 waits=[(g.sPASS, 16 * (2 * T))] if T >= 1 else [],
                 incs=[(g.sPASS, 16)])
            b.op("gpsimd", lambda E, s_=pass_sb, d_=out_d[T * 128:(T + 1) * 128, :]:
                 E.dma_start(out=d_, in_=s_),
                 waits=[(g.sPASS, 16 * (2 * T + 1))], incs=[(g.sPASS, 16)])
        b.wait("sync", (g.sPASS, g.sPASS.n),
               (g.sQTD, g.sQTD.n), (g.sKTD, g.sKTD.n), (g.sVD, g.sVD.n),
               (g.sXNTD, g.sXNTD.n), (g.sXNTLOD, g.sXNTLOD.n),
               (g.sWcast, WCAST_FULL), (g.sCW, g.sCW.n),
               *([(g.sDBG, 16)] if DEBUG else []))

    blk = E_(nc.Block())
    blk.sync(lambda Eng: [fn(Eng) for fn in b.streams["sync"]])
    blk.gpsimd(lambda Eng: [fn(Eng) for fn in b.streams["gpsimd"]])
    blk.tensor(lambda Eng: [fn(Eng) for fn in b.streams["tensor"]])
    blk.vector(lambda Eng: [fn(Eng) for fn in b.streams["vector"]])
    blk.scalar(lambda Eng: [fn(Eng) for fn in b.streams["scalar"]])
    st.close()
    return nc


def kernel(L0_attn_embeddings, L0_attn_W_Q, L0_attn_W_K, L0_attn_W_V,
           L0_ffn_Wg, L0_ffn_W1, L0_ffn_W2, L0_ffn_W3):
    nc = build_nc()
    emb = np.ascontiguousarray(np.asarray(L0_attn_embeddings))
    base = {
        "wq": np.ascontiguousarray(L0_attn_W_Q), "wk": np.ascontiguousarray(L0_attn_W_K),
        "wv": np.ascontiguousarray(L0_attn_W_V), "wg": np.ascontiguousarray(L0_ffn_Wg),
        "w1": np.ascontiguousarray(L0_ffn_W1), "w2": np.ascontiguousarray(L0_ffn_W2),
        "w3": np.ascontiguousarray(L0_ffn_W3),
    }
    in_maps = [dict(base, x=np.ascontiguousarray(emb[i])) for i in range(8)]
    res = run_bass_kernel_spmd(nc, in_maps, core_ids=list(range(8)))
    out = np.stack([np.asarray(r["out"]) for r in res.results], axis=0)
    if DEBUG:
        kernel.debug_results = res.results
    return out


kernel.debug_results = None
